# version 16
# baseline (speedup 1.0000x reference)
import numpy as np
import ml_dtypes
import concourse.bass as bass
import concourse.mybir as mybir
from concourse.bass_utils import run_bass_kernel_spmd

F32 = mybir.dt.float32
BF16 = mybir.dt.bfloat16
I32 = mybir.dt.int32
U8 = mybir.dt.uint8
AF = mybir.ActivationFunctionType
ALU = mybir.AluOpType
AX = mybir.AxisListType
BF = ml_dtypes.bfloat16

DM = 2048
DIN = 4096
NH = 64
DS = 64
CHUNK = 128
EPS = 1e-5
CONV_DIM = DIN + 2 * 1024
DPROJ = 2 * DIN + 2 * 1024 + NH
HID = 8192

T = 1024
PRE = 3
TE_ = T + PRE
NCH = T // CHUNK
NEG = -1e30
DTSZ = {F32: 4, BF16: 2, I32: 4}


class Sems:
    def __init__(self, nc):
        self.nc = nc
        self.h = {}
        self.cnt = {}

    def new(self, *names):
        for name in names:
            self.h[name] = self.nc.alloc_semaphore(name=name)
            self.cnt[name] = 0

    def inc(self, name, by=1):
        self.cnt[name] += by
        return self.h[name], by

    def val(self, name):
        return self.cnt[name]

    def hdl(self, name):
        return self.h[name]


class Arena:
    def __init__(self, nc, name, nbytes):
        self.t = nc.alloc_sbuf_tensor(name, [128, nbytes], U8)
        self.n = nbytes
        self.off = 0
        self.stack = []

    def alloc(self, shape, dtype):
        assert shape[0] <= 128
        elems = 1
        for d in shape[1:]:
            elems *= d
        nb = elems * DTSZ[dtype]
        off = (self.off + 31) & ~31
        assert off + nb <= self.n, f"arena overflow: {off}+{nb} > {self.n}"
        self.off = off + nb
        ap = self.t.ap()[0:shape[0], off:off + nb].bitcast(dtype)
        if len(shape) > 2:
            pat = " ".join(f"a{i}" for i in range(len(shape) - 1))
            ap = ap.rearrange(f"p ({pat}) -> p {pat}",
                              **{f"a{i}": shape[i + 1] for i in range(len(shape) - 1)})
        return ap

    def at(self, off, shape, dtype):
        elems = 1
        for d in shape[1:]:
            elems *= d
        nb = elems * DTSZ[dtype]
        ap = self.t.ap()[0:shape[0], off:off + nb].bitcast(dtype)
        if len(shape) > 2:
            pat = " ".join(f"a{i}" for i in range(len(shape) - 1))
            ap = ap.rearrange(f"p ({pat}) -> p {pat}",
                              **{f"a{i}": shape[i + 1] for i in range(len(shape) - 1)})
        return ap

    def push(self):
        self.stack.append(self.off)

    def pop(self):
        self.off = self.stack.pop()


def build(stop_after=None, debug_taps=(), cstop=None):
    nc = bass.Bass("TRN2", target_bir_lowering=False)

    uT = nc.dram_tensor("uT", [DM, TE_], F32, kind="ExternalInput").ap()
    u_in = nc.dram_tensor("u", [TE_, DM], F32, kind="ExternalInput").ap()
    W_in = nc.dram_tensor("W_in", [DM, DPROJ], F32, kind="ExternalInput").ap()
    conv_w = nc.dram_tensor("conv_w", [128, 48, 4], F32, kind="ExternalInput").ap()
    conv_b = nc.dram_tensor("conv_b", [128, 48], F32, kind="ExternalInput").ap()
    dt_bias = nc.dram_tensor("dt_bias", [NH, 1], F32, kind="ExternalInput").ap()
    A_log = nc.dram_tensor("A_log", [NH, 1], F32, kind="ExternalInput").ap()
    Dv = nc.dram_tensor("Dv", [1, NH], F32, kind="ExternalInput").ap()
    nw_in = nc.dram_tensor("nw", [1, DIN], F32, kind="ExternalInput").ap()
    W_out = nc.dram_tensor("W_out", [DIN, DM], F32, kind="ExternalInput").ap()
    w1_in = nc.dram_tensor("w1", [HID // 8, DM], F32, kind="ExternalInput").ap()
    w2_in = nc.dram_tensor("w2", [DM // 8, HID], F32, kind="ExternalInput").ap()
    sel4 = nc.dram_tensor("sel4", [1, 4], F32, kind="ExternalInput").ap()

    out = nc.dram_tensor("out", [T, DM], F32, kind="ExternalOutput").ap()

    z_dram = nc.dram_tensor("z_dram", [DIN, TE_], BF16).ap()
    s_dram = nc.dram_tensor("s_dram", [NCH, NH, NH * DS], BF16).ap()
    nrm_dram = nc.dram_tensor("nrm_dram", [DIN, T], BF16).ap()
    TCC1 = DM * (HID // 8)
    TCC2 = HID * (DM // 8)
    TCC = TCC1 + TCC2
    tcc_in = nc.dram_tensor("tcc_in", [1, TCC], BF16).ap()
    tcc_out = nc.dram_tensor("tcc_out", [8, TCC], BF16,
                             addr_space="Shared").ap()
    T1V = [tcc_out[g:g + 1, 0:TCC1].rearrange(
        "g (p h4 x) -> (g p) h4 x", p=128, h4=4, x=4096)
        for g in range(8)]
    T2V = [tcc_out[g:g + 1, TCC1:TCC].rearrange(
        "g (p kt dl) -> (g p) kt dl", p=128, kt=64, dl=256)
        for g in range(8)]
    T1I = tcc_in[0:1, 0:TCC1].rearrange("g (p h4 x) -> (g p) h4 x",
                                        p=128, h4=4, x=4096)
    T2I = tcc_in[0:1, TCC1:TCC].rearrange("g (p kt dl) -> (g p) kt dl",
                                          p=128, kt=64, dl=256)
    g_dram = nc.dram_tensor("g_dram", [T, HID], BF16).ap()
    q2_dram = nc.dram_tensor("q2_dram", [T // 128, 2, 128, 4096], BF16).ap()
    cumrow_dram = nc.dram_tensor("cumrow_dram", [1, NH * CHUNK], F32).ap()
    wob_dram = nc.dram_tensor("wob_dram", [DIN, DM], BF16).ap()
    CCH = NH * NH * DS
    cc_in = nc.dram_tensor("cc_in", [1, CCH + NH], F32).ap()
    cc_out = nc.dram_tensor("cc_out", [4, CCH + NH], F32).ap()

    taps = {}

    def tap(name, shape, dtype=F32):
        taps[name] = nc.dram_tensor("tap_" + name, shape, dtype,
                                    kind="ExternalOutput").ap()
        return taps[name]

    sb = lambda name, shape, dtype=F32: nc.alloc_sbuf_tensor(name, shape, dtype)

    cstk = sb("cstk", [128, 8], F32)
    one_col = sb("one_col", [128, 1], F32)
    A_col = sb("A_col", [NH, 1], F32)
    dtb_col = sb("dtb_col", [NH, 1], F32)
    D_rep = sb("D_rep", [128, NH], F32)
    nw_bc = sb("nw_bc", [128, DIN], BF16)
    eye = sb("eye", [128, 128], BF16)
    eyef = sb("eyef", [128, 128], F32)
    msk = sb("msk", [128, 128], F32)
    zeros64 = sb("zeros64", [NH, CHUNK], F32)
    selc = sb("selc", [1, 4], F32)
    omc = sb("omc", [1, 4], F32)
    dt_sb = sb("dt_sb", [NH, T], F32)
    cum = sb("cum", [NH, T], F32)
    ctm = sb("ctm", [128, 6, NH], F32)
    smallq = sb("smallq", [128, 24], F32)
    xbc = None

    CONST = {1.0: 0, 0.5: 1, -0.5: 2, EPS: 3, 1.0 / 64: 4, 1e-5: 5, 127.0: 6,
             1.0 / DM: 7}

    def cc(v, p=128):
        i = CONST[v]
        return cstk[0:p, i:i + 1]

    rem = nc.sbuf_bytes_remaining
    per_part = rem // 128 if rem > 400000 else rem
    AR = Arena(nc, "arena", (per_part - 640) & ~63)

    pb = [nc.alloc_psum_tensor(f"pb{i}", [128, 512], F32) for i in range(4)]
    ptr_b = nc.alloc_psum_tensor("ptr_b", [128, 512], BF16)
    ptr_f = nc.alloc_psum_tensor("ptr_f", [128, 128], F32)
    pss = nc.alloc_psum_tensor("pss", [128, 512], F32)
    pgg = nc.alloc_psum_tensor("pgg", [128, 512], F32)

    S = Sems(nc)
    S.new("q_w", "q_u", "q_z", "q_misc", "q_tw", "q_tt", "q_sd", "q_cc",
          "q_cb", "q_nr", "q_wo", "q_ur", "q_t1", "q_gp", "q_q2", "q_t2",
          "q_q2l", "q_out", "q_zr", "q_dr", "q_hg", "q_gr", "q_gbc", "q_tap",
          "q_wob", "q_t2b", "q_q2lb", "q_wob2", "q_grb", "q_twb")
    S.new("g_set", "v_set", "s_set", "te_a", "ve_a", "cv", "cnv_i1", "cnv_v",
          "dt_v", "dt_s", "v_tern", "g_tern", "t_tern", "s_tern",
          "b1_s", "b1_v", "b1_t", "cc_done", "hin_v",
          "b2_s", "b2_v", "b2_t", "b2_gm", "b2_ye", "b2_zs", "b2_xt",
          "b2_hb", "b2_nm",
          "c_v", "c_s", "c_t", "d_v", "d_s", "d_t", "d_gm", "pbe", "pfe", "fin",
          "cc2")

    holder = {}

    class PsumRing:
        def __init__(self, sem, nslots):
            self.sem = sem
            self.n = nslots
            self.ev = [0] * nslots
            self.ctr = 0

        def acquire(self):
            sl_ = self.ctr % self.n
            self.ctr += 1
            return sl_, self.ev[sl_]

        def release(self, sl_):
            self.ev[sl_] = S.val(self.sem)

    PB = PsumRing("pbe", 1)
    PF = PsumRing("pfe", 1)

    def DMA(e, q, out, in_):
        if S.val(q):
            e.wait_ge(S.hdl(q), S.val(q))
        return e.dma_start(out=out, in_=in_).then_inc(*S.inc(q, 16))

    def G(fn): holder["b"].gpsimd(fn)
    def V(fn): holder["b"].vector(fn)
    def SC(fn): holder["b"].scalar(fn)
    def TEE(fn): holder["b"].tensor(fn)
    def SY(fn): holder["b"].sync(fn)

    JBS = []
    _j = DIN
    while _j < DPROJ:
        JBS.append((_j, min(512, DPROJ - _j)))
        _j += 512
    _j = 0
    while _j < DIN:
        JBS.append((_j, min(512, DIN - _j)))
        _j += 512

    def nsub(w):
        return (w + 127) // 128

    def ttiles_for(jg0):
        if DIN <= jg0 < DIN + CONV_DIM:
            return [(0, 512), (512, 512), (1024, PRE)]
        return [(PRE, 512), (PRE + 512, 512)]

    a_groups = []
    grp_upto_bi = {}
    for bi, (jb0, w) in enumerate(JBS):
        for si in range(nsub(w)):
            jw = min(128, w - si * 128)
            for (t0, tn) in ttiles_for(jb0 + si * 128):
                a_groups.append((bi, jb0 + si * 128, jw, t0, tn))
        grp_upto_bi[bi] = len(a_groups)

    def emit(stop_after):
        nonlocal xbc
        AR.push()
        xbc = AR.alloc([128, 48, TE_], BF16)
        AR.push()
        uT_sb = AR.alloc([128, 16, TE_], BF16)
        wbuf = AR.alloc([128, 2, 16, 512], BF16)
        zb = AR.alloc([128, 2, 512], BF16)
        dtraw = AR.alloc([NH, TE_], F32)
        dts = AR.alloc([NH, 2, T], F32)
        convw_sb = AR.alloc([128, 48, 4], F32)
        convb_sb = AR.alloc([128, 48], F32)
        acc = AR.alloc([128, 2, T], F32)

        def g_fn(g):
            for v, i in CONST.items():
                g.memset(cstk[:, i:i + 1], float(v))
            g.memset(one_col[:], 1.0)
            g.memset(zeros64[:], 0.0)
            g.memset(eye[:], 0.0)
            g.memset(msk[:], 0.0)
            g.drain()
            g.affine_select(out=eye[:], in_=eye[:], compare_op=ALU.not_equal,
                            fill=1.0, base=0, pattern=[[-1, 128]],
                            channel_multiplier=1)
            g.affine_select(out=msk[:], in_=msk[:], compare_op=ALU.is_ge,
                            fill=NEG, base=0, pattern=[[1, 128]],
                            channel_multiplier=-1)
            g.drain().then_inc(*S.inc("g_set"))
            uT3 = uT.rearrange("(kt p) t -> p kt t", p=128)
            DMA(g, "q_u", uT_sb[:, 0:8, :], uT3[:, 0:8, :])
            g.nop()
            DMA(g, "q_u", uT_sb[:, 8:16, :], uT3[:, 8:16, :])
            g.nop()
            DMA(g, "q_gbc", nw_bc[:], nw_in.unsqueeze(1).broadcast_to([1, 128, DIN]))
            DMA(g, "q_gbc", D_rep[:], Dv.unsqueeze(1).broadcast_to([1, 128, NH]))
        G(g_fn)

        def y_fn(y):
            DMA(y, "q_misc", dtb_col[:], dt_bias)
            DMA(y, "q_misc", A_col[:], A_log)
            DMA(y, "q_misc", selc[:], sel4)
            DMA(y, "q_misc", convw_sb[:], conv_w)
            DMA(y, "q_misc", convb_sb[:], conv_b)
        SY(y_fn)
        misc = S.val("q_misc")

        def s_fn(s):
            s.wait_ge(S.hdl("q_misc"), misc)
            s.activation(A_col[:], A_col[:], AF.Exp)
            s.drain().then_inc(*S.inc("s_set"))
        SC(s_fn)
        gv, sv = S.val("g_set"), S.val("s_set")

        def v_fn(v):
            v.wait_ge(S.hdl("g_set"), gv)
            v.tensor_copy(eyef[:], eye[:])
            v.wait_ge(S.hdl("s_set"), sv)
            v.tensor_scalar_mul(A_col[:], A_col[:], -1.0)
            v.wait_ge(S.hdl("q_misc"), misc)
            v.tensor_scalar(omc[0:1, :], selc[:], -1.0, 1.0, ALU.mult, ALU.add)
            v.drain().then_inc(*S.inc("v_set"))
        V(v_fn)
        vset = S.val("v_set")

        def g_fn2(g):
            W3 = W_in.rearrange("(kt p) j -> p kt j", p=128)
            for bi, (jb0, w) in enumerate(JBS):
                if bi >= 2:
                    g.wait_ge(S.hdl("te_a"), grp_upto_bi[bi - 2])
                DMA(g, "q_w", wbuf[:, bi % 2, 0:8, 0:w], W3[:, 0:8, jb0:jb0 + w])
                g.nop()
                DMA(g, "q_w", wbuf[:, bi % 2, 8:16, 0:w], W3[:, 8:16, jb0:jb0 + w])
                g.nop()
        G(g_fn2)

        def t_fn(t):
            t.wait_ge(S.hdl("q_u"), 32)
            t.wait_ge(S.hdl("v_set"), vset)
            for idx, (bi, jg0, jw, t0, tn) in enumerate(a_groups):
                if idx == 0 or a_groups[idx - 1][0] != bi:
                    t.wait_ge(S.hdl("q_w"), 32 * (bi + 1))
                if idx >= 4:
                    t.wait_ge(S.hdl("ve_a"), idx - 3)
                bank = pb[idx % 4]
                jl0 = jg0 - JBS[bi][0]
                for k in range(16):
                    ins = t.matmul(bank[0:jw, 0:tn],
                                   wbuf[:, bi % 2, k, jl0:jl0 + jw],
                                   uT_sb[:, k, t0:t0 + tn],
                                   start=(k == 0), stop=(k == 15))
                ins.then_inc(*S.inc("te_a"))
        TEE(t_fn)

        zdma = []
        nz_total = sum(1 for (bi, jg0, jw, t0, tn) in a_groups if jg0 < DIN)

        def v_fn2(v):
            nz = 0
            for idx, (bi, jg0, jw, t0, tn) in enumerate(a_groups):
                v.wait_ge(S.hdl("te_a"), idx + 1)
                bank = pb[idx % 4]
                if jg0 < DIN:
                    if nz >= 2:
                        v.wait_ge(S.hdl("q_z"), 16 * (nz - 1))
                    v.tensor_copy(zb[0:jw, nz % 2, 0:tn], bank[0:jw, 0:tn])
                    v.drain().then_inc(*S.inc("ve_a"))
                    zdma.append((S.val("ve_a"), nz % 2, jg0, jw, t0, tn))
                    nz += 1
                elif jg0 < DIN + CONV_DIM:
                    ct = (jg0 - DIN) // 128
                    v.tensor_copy(xbc[0:jw, ct, t0:t0 + tn], bank[0:jw, 0:tn])
                    v.drain().then_inc(*S.inc("ve_a"))
                else:
                    v.tensor_copy(dtraw[0:jw, t0:t0 + tn], bank[0:jw, 0:tn])
                    v.drain().then_inc(*S.inc("ve_a"))
        V(v_fn2)

        def y_fn2(y):
            for (vev, par, jg0, jw, t0, tn) in zdma:
                y.wait_ge(S.hdl("ve_a"), vev)
                DMA(y, "q_z", z_dram[jg0:jg0 + jw, t0:t0 + tn], zb[0:jw, par, 0:tn])
        SY(y_fn2)

        ct_ready = {}
        dt_ready = 0
        for idx, (bi, jg0, jw, t0, tn) in enumerate(a_groups):
            if DIN <= jg0 < DIN + CONV_DIM:
                ct_ready[(jg0 - DIN) // 128] = idx + 1
            if jg0 >= DIN + CONV_DIM:
                dt_ready = max(dt_ready, idx + 1)

        def s_fn2(s):
            s.wait_ge(S.hdl("q_misc"), misc)
            for ct in range(48):
                s.wait_ge(S.hdl("ve_a"), ct_ready[ct])
                if ct >= 2:
                    s.wait_ge(S.hdl("cnv_v"), ct - 1)
                s.activation(acc[:, ct % 2, :], xbc[:, ct, 0:T], AF.Identity,
                             bias=convb_sb[:, ct:ct + 1],
                             scale=convw_sb[:, ct, 0:1])
                s.drain().then_inc(*S.inc("cnv_i1"))
                s.wait_ge(S.hdl("cnv_v"), ct + 1)
                s.activation(xbc[:, ct, 0:T], acc[:, ct % 2, :], AF.Silu)
                s.drain().then_inc(*S.inc("cv"))
        SC(s_fn2)

        def v_fn3(v):
            for ct in range(48):
                v.wait_ge(S.hdl("cnv_i1"), ct + 1)
                for k in (1, 2, 3):
                    v.scalar_tensor_tensor(acc[:, ct % 2, :],
                                           xbc[:, ct, k:k + T],
                                           convw_sb[:, ct, k:k + 1],
                                           acc[:, ct % 2, :], ALU.mult, ALU.add)
                    v.drain()
                v.nop().then_inc(*S.inc("cnv_v"))
        V(v_fn3)

        def v_fn4(v):
            v.wait_ge(S.hdl("ve_a"), dt_ready)
            v.tensor_scalar(dt_sb[:], dtraw[:, PRE:PRE + T], dtb_col[:], None,
                            ALU.add)
            v.drain().then_inc(*S.inc("dt_v"))
            v.wait_ge(S.hdl("dt_s"), 1)
            v.tensor_scalar(dts[:, 0, :], dt_sb[:], 0.0, None, ALU.max)
            v.drain()
            v.tensor_add(dt_sb[:], dts[:, 0, :], dts[:, 1, :])
            v.drain()
            v.tensor_scalar(dts[:, 0, :], dt_sb[:], A_col[:], None, ALU.mult)
            v.drain()
            for c in range(NCH):
                v.tensor_tensor_scan(cum[:, c * CHUNK:(c + 1) * CHUNK],
                                     dts[:, 0, c * CHUNK:(c + 1) * CHUNK],
                                     zeros64[:], 0.0, ALU.add, ALU.add)
            v.drain().then_inc(*S.inc("dt_v"))
        V(v_fn4)

        def s_fn3(s):
            s.wait_ge(S.hdl("dt_v"), 1)
            s.activation(dts[:, 0, :], dt_sb[:], AF.Abs)
            s.drain()
            s.activation(dts[:, 1, :], dts[:, 0, :], AF.Exp, scale=-1.0)
            s.drain()
            s.activation(dts[:, 1, :], dts[:, 1, :], AF.Ln,
                         bias=one_col[0:NH, :], scale=1.0)
            s.drain().then_inc(*S.inc("dt_s"))
        SC(s_fn3)

        if "xs" in debug_taps:
            tp1 = tap("xs", [128, 48, T], BF16)
            tp2 = tap("dt", [NH, T])
            tp3 = tap("cum", [NH, T])
            cvv, dvv = 48, 2

            def g_tap(g):
                g.wait_ge(S.hdl("cv"), cvv)
                g.wait_ge(S.hdl("dt_v"), dvv)
                DMA(g, "q_tap", tp1, xbc[:, :, 0:T])
                DMA(g, "q_tap", tp2, dt_sb[:])
                DMA(g, "q_tap", tp3, cum[:])
            G(g_tap)

        AR.pop()

        AR.push()
        twk = AR.alloc([128, 4, 2048], F32)
        sga = AR.alloc([128, 2, 2048], BF16)
        sgb = AR.alloc([128, 2, 2048], BF16)
        ttn = AR.alloc([128, 2, 2048], BF16)
        tdg = AR.alloc([128, 2, 128], BF16)
        tsc = AR.alloc([128, 8], F32)
        tev1 = AR.alloc([128, 16, 256], BF16)
        tev2 = AR.alloc([128, 64, 128], BF16)

        tern_groups = [("w1", [u]) for u in range(8)] + \
                      [("w2", [8, 9, 10, 11]), ("w2", [12, 13, 14, 15])]
        sign_done = {}
        add_done = {}
        te_done = {}
        evac_done = {}
        glast_te = {}
        store_qtt = {}

        for gi, (kind, units) in enumerate(tern_groups):
            DW = DM if kind == "w1" else HID
            gpar = gi % 2
            for u in units:
                qtwq = "q_tw" if u % 2 == 0 else "q_twb"

                def g_ld(g, u=u, kind=kind, tea=S.val("te_a"), qtwq=qtwq,
                         qz=S.val("q_z"),
                         free=(sign_done.get(u - 4, 0))):
                    if u == 0:
                        g.wait_ge(S.hdl("te_a"), tea)
                        g.wait_ge(S.hdl("q_z"), qz)
                        g.wait_ge(S.hdl("dt_v"), 2)
                        g.wait_ge(S.hdl("cv"), 48)
                    if u >= 4:
                        g.wait_ge(S.hdl("s_tern"), free)
                    if kind == "w1":
                        src_ap = w1_in[u * 128:u * 128 + 128, :]
                    else:
                        gg, j = (u - 8) // 4, (u - 8) % 4
                        src_ap = w2_in[gg * 128:gg * 128 + 128,
                                       j * 2048:(j + 1) * 2048]
                    DMA(g, qtwq, twk[:, u % 4, :], src_ap)
                G(g_ld)
            qtw = (S.val("q_tw"), S.val("q_twb"))

            def v_red(v, kind=kind, units=units, qtw=qtw, DW=DW, gpar=gpar,
                      tprev=glast_te.get(gi - 2, 0)):
                v.wait_ge(S.hdl("q_tw"), qtw[0])
                if qtw[1]:
                    v.wait_ge(S.hdl("q_twb"), qtw[1])
                for i, u in enumerate(units):
                    col = 0 if kind == "w1" else 4 + i
                    v.reduce_sum(tsc[:, col:col + 1], twk[:, u % 4, :], AX.X,
                                 apply_absolute_value=True)
                v.drain()
                if kind == "w2":
                    v.tensor_add(tsc[:, 0:1], tsc[:, 4:5], tsc[:, 5:6])
                    v.tensor_add(tsc[:, 1:2], tsc[:, 6:7], tsc[:, 7:8])
                    v.drain()
                    v.tensor_add(tsc[:, 0:1], tsc[:, 0:1], tsc[:, 1:2])
                    v.drain()
                v.tensor_scalar(tsc[:, 1:2], tsc[:, 0:1], 1.0 / DW, 1e-5,
                                ALU.mult, ALU.max)
                v.drain()
                v.reciprocal(tsc[:, 2:3], tsc[:, 1:2])
                v.tensor_scalar(tsc[:, 3:4], tsc[:, 1:2], 0.5, None, ALU.mult)
                v.drain()
                v.wait_ge(S.hdl("t_tern"), tprev)
                v.tensor_scalar(tdg[:, gpar, :], eye[:], tsc[:, 3:4], None,
                                ALU.mult)
                v.drain().then_inc(*S.inc("v_tern"))
            V(v_red)
            scale_done = S.val("v_tern")

            for u in units:
                par = u % 2

                def s_sg(s, u=u, par=par, scale_done=scale_done,
                         abfree=add_done.get(u - 2, 0)):
                    s.wait_ge(S.hdl("v_tern"), scale_done)
                    if u >= 2:
                        s.wait_ge(S.hdl("v_tern"), abfree)
                    s.activation(sga[:, par, :], twk[:, u % 4, :], AF.Sign,
                                 bias=cc(-0.5), scale=tsc[:, 2:3])
                    ins = s.activation(sgb[:, par, :], twk[:, u % 4, :],
                                       AF.Sign, bias=cc(0.5),
                                       scale=tsc[:, 2:3])
                    ins.then_inc(*S.inc("s_tern"))
                SC(s_sg)
                sign_done[u] = S.val("s_tern")

                def v_add(v, u=u, par=par, sd=sign_done[u],
                          tfree=te_done.get((u - 2, 3), 0)):
                    v.wait_ge(S.hdl("s_tern"), sd)
                    if u >= 2:
                        v.wait_ge(S.hdl("t_tern"), tfree)
                    v.tensor_add(ttn[:, par, :], sga[:, par, :],
                                 sgb[:, par, :])
                    v.drain().then_inc(*S.inc("v_tern"))
                V(v_add)
                add_done[u] = S.val("v_tern")

                for q in range(4):
                    gq = u * 4 + q
                    bank = pgg if gq % 2 == 0 else pss

                    def t_tr(t, u=u, q=q, par=par, gpar=gpar, bank=bank,
                             ad=add_done[u], ev=evac_done.get(gq - 2, 0)):
                        t.wait_ge(S.hdl("v_tern"), max(ad, ev))
                        for d in range(4):
                            ins = t.matmul(
                                bank[0:128, d * 128:d * 128 + 128],
                                ttn[:, par, (q * 4 + d) * 128:
                                    (q * 4 + d) * 128 + 128],
                                tdg[:, gpar, :], start=True, stop=True)
                        ins.then_inc(*S.inc("t_tern"))
                    TEE(t_tr)
                    te_done[(u, q)] = S.val("t_tern")

                    if u < 8:
                        stgate = store_qtt.get((u - 2) // 2, 0) \
                            if u >= 2 else 0
                    else:
                        stgate = store_qtt.get(("t2", 0), 0) \
                            if u >= 12 else 0

                    def v_ev(v, u=u, q=q, bank=bank, td=te_done[(u, q)],
                             stgate=stgate):
                        v.wait_ge(S.hdl("t_tern"), td)
                        if stgate:
                            v.wait_ge(S.hdl("q_tt"), stgate)
                        if u < 8:
                            dest = tev1[:, q * 4:q * 4 + 4,
                                        (u % 2) * 128:(u % 2) * 128 + 128]
                        else:
                            j = (u - 8) % 4
                            dest = tev2[:, j * 16 + q * 4:j * 16 + q * 4 + 4,
                                        :]
                        v.tensor_copy(dest,
                                      bank.ap().rearrange("p (a b) -> p a b",
                                                          a=4))
                        v.drain().then_inc(*S.inc("v_tern"))
                    V(v_ev)
                    evac_done[gq] = S.val("v_tern")
            glast_te[gi] = S.val("t_tern")

            if kind == "w1" and units[0] % 2 == 1:
                pair = units[0] // 2
                evs = S.val("v_tern")

                def y_st1(y, pair=pair, evs=evs):
                    y.wait_ge(S.hdl("v_tern"), evs)
                    DMA(y, "q_tt", T1I[:, pair, :],
                        tev1.rearrange("p a b -> p (a b)"))
                    y.nop()
                SY(y_st1)
                store_qtt[pair] = S.val("q_tt")
            if kind == "w2":
                gg = 0 if units[0] == 8 else 1
                evs = S.val("v_tern")

                def y_st2(y, gg=gg, evs=evs):
                    y.wait_ge(S.hdl("v_tern"), evs)
                    for b in range(8):
                        DMA(y, "q_tt",
                            T2I[:, b * 8:b * 8 + 8,
                                gg * 128:gg * 128 + 128],
                            tev2[:, b * 8:b * 8 + 8, :])
                        y.nop()
                SY(y_st2)
                store_qtt[("t2", gg)] = S.val("q_tt")

        if "tern" in debug_taps:
            tpt = tap("tern", [1, TCC], BF16)
            qtt = S.val("q_tt")

            def g_tapt(g, qtt=qtt):
                g.wait_ge(S.hdl("q_tt"), qtt)
                DMA(g, "q_tap", tpt, tcc_in)
            G(g_tapt)
        AR.pop()

        if stop_after == "A":
            return

        AR.push()
        Hbuf = AR.alloc([NH, NH * DS], F32)
        Hbf = AR.alloc([NH, NH * DS], BF16)
        Schk = AR.alloc([NH, NH * DS], BF16)
        cumb = AR.alloc([128, 32 * CHUNK], F32)
        zraw = AR.alloc([128, 32, 128], BF16)
        zs_tm = AR.alloc([128, 32, 128], BF16)
        yzc = AR.alloc([128, NH * DS], BF16)
        nrmc = yzc
        sqp = AR.alloc([128, 2, 512], F32)
        nrf4 = AR.alloc([128, 2, 4, 128], BF16)
        xdt_tm = AR.alloc([128, 8, 64], BF16)
        bdec_tm = AR.alloc([128, 8, 64], BF16)
        xtm = AR.alloc([128, 128], BF16)
        bsc = AR.alloc([64, 2, 128], BF16)
        csc = AR.alloc([64, 2, 128], BF16)
        Xt = AR.alloc([128, 2, 2, 128], F32)
        Mt = AR.alloc([128, 2, 2, 128], F32)
        gmb = AR.alloc([128, 2, 2, 128], BF16)
        tcor = None
        yv = AR.alloc([128, 128], F32)
        yv2 = AR.alloc([128, 128], F32)
        dcyf = AR.alloc([NH, CHUNK], F32)
        deccol = AR.alloc([NH, 1], F32)
        dagg = AR.alloc([NH, 1], F32)
        drow = AR.alloc([1, NH], F32)
        drep = AR.alloc([NH, NH], F32)
        row64 = AR.alloc([1, NH], F32)
        erep = AR.alloc([NH, NH], F32)
        crep = AR.alloc([NH, 1], F32)
        zsl = AR.alloc([128, 4, 128], BF16)

        emit_B1(Hbuf, Schk, dcyf, deccol, dagg, drow, drep, xdt_tm, bdec_tm)

        def g_cc2(g, qtt=S.val("q_tt")):
            g.wait_ge(S.hdl("q_tt"), qtt)
            g.collective_compute(
                "AllGather", ALU.bypass, ins=[tcc_in], outs=[tcc_out],
                replica_groups=[[0, 1, 2, 3, 4, 5, 6, 7]],
            ).then_inc(*S.inc("cc2"))
        G(g_cc2)

        wcast = AR.alloc([128, 1, DM // 4], BF16)

        def g_wcast(g):
            W3 = W_out.rearrange("(kt p) d -> p kt d", p=128)
            W3b = wob_dram.rearrange("(kt p) d -> p kt d", p=128)
            for i in range(128):
                kt, half = i // 4, (i % 4) * (DM // 4)
                DMA(g, "q_wob", wcast[:, :, :],
                    W3[:, kt:kt + 1, half:half + DM // 4])
                g.nop()
                DMA(g, "q_wob", W3b[:, kt:kt + 1, half:half + DM // 4],
                    wcast[:, :, :])
                g.nop()
        G(g_wcast)

        emit_hin(Hbuf, cumb, row64, erep, crep)
        if "hin" in debug_taps:
            tph = tap("hin", [NH, NH * DS])
            hv = S.val("hin_v")

            def g_taph(g, hv=hv):
                g.wait_ge(S.hdl("hin_v"), hv)
                DMA(g, "q_tap", tph, Hbuf[:])
            G(g_taph)
        if stop_after != "B1":
            emit_B2(Hbuf, Hbf, Schk, cumb, zraw, zs_tm, zsl, yzc,
                    nrmc, sqp, nrf4, xdt_tm, xtm, bsc, csc, Xt, Mt, gmb,
                    tcor, yv, yv2, deccol, drow, drep)
        def g_b2fin(g, waits=[(n, S.val(n)) for n in
                              ("b2_hb", "b2_ye", "b2_nm", "b2_zs", "b2_s",
                               "b2_t", "b2_v", "b2_gm", "b2_xt", "q_nr",
                               "q_sd", "q_dr", "q_cb", "q_zr")]):
            for (n, v) in waits:
                if v:
                    g.wait_ge(S.hdl(n), v)
            g.drain().then_inc(*S.inc("fin"))
        G(g_b2fin)
        AR.pop()
        AR.pop()
        if stop_after in ("B1", "B2"):
            return

        AR.push()
        r2 = AR.alloc([128, 8, DM], F32)
        xqf = AR.alloc([128, 2, 16, 512], BF16)
        AR.push()
        nrh = AR.alloc([128, 32, 512], BF16)
        q1y = AR.alloc([128, DM], F32)
        q1i = AR.alloc([128, DM], I32)
        q1r = AR.alloc([128, DM], F32)
        wob = AR.alloc([128, 2, 8, 512], BF16)
        xq_tm = AR.alloc([128, DM], F32)
        emit_C(r2, nrh, wob, q1y, q1i, q1r, xq_tm, xqf, cstop)
        AR.pop()
        if stop_after == "C":
            return
        AR.push()
        emit_D(r2, xqf)
        AR.pop()
        AR.pop()

    def emit_B1(Hbuf, Schk, dcyf, deccol, dagg, drow, drep, xdt_tm, bdec_tm):
        def g_init(g, vt=S.val("v_tern"), tt=S.val("t_tern"),
                   st_=S.val("s_tern"), qtt=S.val("q_tt"),
                   qz=S.val("q_z")):
            g.wait_ge(S.hdl("cv"), 48)
            g.wait_ge(S.hdl("q_z"), qz)
            g.wait_ge(S.hdl("dt_v"), 2)
            g.wait_ge(S.hdl("v_tern"), vt)
            g.wait_ge(S.hdl("t_tern"), tt)
            g.wait_ge(S.hdl("s_tern"), st_)
            g.wait_ge(S.hdl("q_tt"), qtt)
            g.memset(Hbuf[:], 0.0)
            g.memset(dagg[:], 1.0)
            g.drain().then_inc(*S.inc("b1_v"))
        G(g_init)
        b1v0 = S.val("b1_v")

        def t_gate(t, b1v0=b1v0):
            t.wait_ge(S.hdl("b1_v"), b1v0)
        TEE(t_gate)

        def v_gate(v, b1v0=b1v0):
            v.wait_ge(S.hdl("b1_v"), b1v0)
        V(v_gate)

        def s_gate(s, b1v0=b1v0):
            s.wait_ge(S.hdl("b1_v"), b1v0)
        SC(s_gate)

        def y_gate(y, b1v0=b1v0):
            y.wait_ge(S.hdl("b1_v"), b1v0)
        SY(y_gate)

        prev_sev = 0
        for c in range(NCH):
            sl = slice(c * CHUNK, (c + 1) * CHUNK)
            sc_b1 = []
            inc_d1 = S.inc("b1_s")

            def s_dec(s, c=c, sl=sl, inc=inc_d1, qdr0=S.val("q_dr")):
                if c == 0:
                    s.wait_ge(S.hdl("dt_v"), 2)
                    s.wait_ge(S.hdl("b1_v"), 1)
                else:
                    s.wait_ge(S.hdl("q_dr"), qdr0)
                s.activation(deccol[:], cum[:, sl.stop - 1:sl.stop], AF.Exp)
                s.drain()
                s.activation(dcyf[:], cum[:, sl], AF.Exp, scale=-1.0,
                             bias=cum[:, sl.stop - 1:sl.stop]).then_inc(*inc)
            sc_b1.append(s_dec)
            sdec = S.val("b1_s")

            def v_dagg(v, sdec=sdec):
                v.wait_ge(S.hdl("b1_s"), sdec)
                v.tensor_mul(dagg[:], dagg[:], deccol[:])
                v.drain().then_inc(*S.inc("b1_v"))
            V(v_dagg)

            def y_drep(y, sdec=sdec):
                y.wait_ge(S.hdl("b1_s"), sdec)
                DMA(y, "q_dr", drow[:], deccol[:])
                y.wait_ge(S.hdl("q_dr"), S.val("q_dr"))
                DMA(y, "q_dr", drep[:], drow.unsqueeze(1).broadcast_to([1, NH, NH]))
            SY(y_drep)
            qdr = S.val("q_dr")

            _, wf1 = PF.acquire()

            def t_dt(t, c=c, sl=sl, sdec=sdec, wf1=wf1):
                if c == 0:
                    t.wait_ge(S.hdl("cv"), 48)
                    t.wait_ge(S.hdl("dt_v"), 2)
                t.wait_ge(S.hdl("b1_s"), sdec)
                t.wait_ge(S.hdl("pfe"), wf1)
                t.transpose(ptr_f[0:128, 0:64], dt_sb[0:NH, sl],
                            eyef[0:NH, 0:NH]).then_inc(*S.inc("b1_t"))
            TEE(t_dt)
            t1v = S.val("b1_t")

            def v_dt(v, t1v=t1v):
                v.wait_ge(S.hdl("b1_t"), t1v)
                v.tensor_copy(ctm[:, 2, :], ptr_f[0:128, 0:64]
                              ).then_inc(*S.inc("b1_v"))
                v.drain().then_inc(*S.inc("pfe"))
            V(v_dt)
            vdt_ = S.val("b1_v")
            PF.release(0)
            v2v = S.val("b1_v")
            _, wf2 = PF.acquire()

            def t_dcy(t, v2v=v2v, wf2=wf2):
                t.wait_ge(S.hdl("b1_v"), v2v)
                t.wait_ge(S.hdl("pfe"), wf2)
                t.transpose(ptr_f[0:128, 0:64], dcyf[0:NH, :],
                            eyef[0:NH, 0:NH]).then_inc(*S.inc("b1_t"))
            TEE(t_dcy)
            t2v = S.val("b1_t")

            def v_dcy(v, t2v=t2v, vdt_=vdt_):
                v.wait_ge(S.hdl("b1_t"), t2v)
                v.tensor_copy(ctm[:, 3, :], ptr_f[0:128, 0:64]
                              ).then_inc(*S.inc("b1_v"))
                v.drain().then_inc(*S.inc("pfe"))
            V(v_dcy)
            PF.release(0)
            v3v = S.val("b1_v")

            for xt in range(8):
                bt = 8 + xt
                _, wx = PB.acquire()

                def t_x(t, c=c, sl=sl, xt=xt, bt=bt, v3v=v3v, wx=wx,
                        psev=prev_sev):
                    t.wait_ge(S.hdl("b1_v"), v3v)
                    t.wait_ge(S.hdl("pbe"), wx)
                    if psev:
                        t.wait_ge(S.hdl("b1_s"), psev)
                    t.transpose(ptr_b[0:128, 0:128], xbc[:, xt, sl], eye[:])
                    t.transpose(ptr_b[0:128, 128:256], xbc[:, bt, sl], eye[:]
                                ).then_inc(*S.inc("b1_t"))
                TEE(t_x)
                t3v = S.val("b1_t")

                inc_e1 = S.inc("b1_s")
                inc_e2 = S.inc("pbe")

                def s_ev(s, xt=xt, t3v=t3v, inc_a=inc_e1, inc_b=inc_e2):
                    s.wait_ge(S.hdl("b1_t"), t3v)
                    for hh in range(8):
                        h = 8 * xt + hh
                        xcol = (hh // 4) * 64
                        s.activation(xdt_tm[:, hh, :],
                                     ptr_b[0:128, xcol:xcol + 64], AF.Copy,
                                     scale=ctm[:, 2, h:h + 1])
                        s.activation(bdec_tm[:, hh, :],
                                     ptr_b[0:128, 128 + xcol:128 + xcol + 64],
                                     AF.Copy, scale=ctm[:, 3, h:h + 1])
                    s.drain().then_inc(*inc_a)
                    s.drain().then_inc(*inc_b)
                sc_b1.append(s_ev)
                prev_sev = S.val("b1_s")
                s2v = S.val("b1_s")

                def t_smm(t, s2v=s2v):
                    t.wait_ge(S.hdl("b1_s"), s2v)
                    t.wait_ge(S.hdl("b1_v"), S.cnt["b1_v"])
                    for hh in range(8):
                        ins = t.matmul(pss[0:64, hh * 64:hh * 64 + 64],
                                       bdec_tm[:, hh, :], xdt_tm[:, hh, :],
                                       start=True, stop=True)
                    ins.then_inc(*S.inc("b1_t"))
                TEE(t_smm)
                t4v = S.val("b1_t")

                def v_srec(v, c=c, xt=xt, t4v=t4v, qdr=qdr,
                           qsd0=S.val("q_sd")):
                    v.wait_ge(S.hdl("b1_t"), t4v)
                    if xt == 0:
                        v.wait_ge(S.hdl("q_sd"), qsd0)
                        v.wait_ge(S.hdl("q_dr"), qdr)
                    v.tensor_copy(Schk[:, xt * 512:(xt + 1) * 512], pss[0:64, :])
                    for hh in range(8):
                        h = 8 * xt + hh
                        ins = v.scalar_tensor_tensor(
                            Hbuf[:, h * 64:h * 64 + 64],
                            Hbuf[:, h * 64:h * 64 + 64],
                            drep[:, h:h + 1],
                            pss[0:64, hh * 64:hh * 64 + 64],
                            ALU.mult, ALU.add)
                    ins.then_inc(*S.inc("b1_v"))
                V(v_srec)

            def s_b1chunk(s, ops=list(sc_b1)):
                for f in ops:
                    f(s)
            SC(s_b1chunk)
            vS = S.val("b1_v")

            def y_schk(y, c=c, vS=vS):
                y.wait_ge(S.hdl("b1_v"), vS)
                DMA(y, "q_sd", s_dram[c], Schk[:])
            SY(y_schk)

        vH = S.val("b1_v")
        qsd = S.val("q_sd")

        def y_cc(y, vH=vH, qsd=qsd):
            y.wait_ge(S.hdl("b1_v"), vH)
            y.wait_ge(S.hdl("q_sd"), qsd)
            DMA(y, "q_cc", cc_in[0:1, 0:CCH], Hbuf[:])
            DMA(y, "q_cc", cc_in[0:1, CCH:], dagg[:])
        SY(y_cc)
        qcc = S.val("q_cc")

        if "cc" in debug_taps:
            tpc = tap("cc", [1, CCH + NH])

            def g_tapc(g, qcc=qcc):
                g.wait_ge(S.hdl("q_cc"), qcc)
                DMA(g, "q_tap", tpc, cc_in)
            G(g_tapc)

        def g_cc(g, qcc=qcc):
            g.wait_ge(S.hdl("q_cc"), qcc)
            g.collective_compute(
                "AllGather", ALU.bypass,
                ins=[cc_in],
                outs=[cc_out],
                replica_groups=[[0, 1, 2, 3], [4, 5, 6, 7]],
            ).then_inc(*S.inc("cc_done"))
        G(g_cc)

    def emit_hin(Hbuf, HgF, row64, erep, crep):
        ccd = S.val("cc_done")

        def v_zero(v, ccd=ccd):
            v.wait_ge(S.hdl("cc_done"), ccd)
            v.memset(Hbuf[:], 0.0)
            v.drain().then_inc(*S.inc("hin_v"))
        V(v_zero)

        for m in range(4):
            hz = S.val("hin_v")

            def y_ld(y, m=m, ccd=ccd, hz=hz):
                y.wait_ge(S.hdl("cc_done"), ccd)
                y.wait_ge(S.hdl("hin_v"), hz)
                DMA(y, "q_hg", HgF[0:NH, 0:CCH // NH], cc_out[m:m + 1, 0:CCH])
                DMA(y, "q_hg", row64[:], cc_out[m:m + 1, CCH:])
                DMA(y, "q_hg", crep[:], selc[0:1, m:m + 1].unsqueeze(1)
                            .broadcast_to([1, NH, 1]))
            SY(y_ld)
            qhg = S.val("q_hg")

            def v_e(v, m=m, qhg=qhg):
                v.wait_ge(S.hdl("q_hg"), qhg)
                v.tensor_scalar(row64[:], row64[:], selc[0:1, m:m + 1],
                                omc[0:1, m:m + 1], ALU.mult, ALU.add)
                v.drain().then_inc(*S.inc("hin_v"))
            V(v_e)
            hv = S.val("hin_v")

            def y_erep(y, hv=hv):
                y.wait_ge(S.hdl("hin_v"), hv)
                DMA(y, "q_hg", erep[:], row64.unsqueeze(1).broadcast_to([1, NH, NH]))
            SY(y_erep)
            qhg2 = S.val("q_hg")

            def v_comb(v, m=m, qhg2=qhg2):
                v.wait_ge(S.hdl("q_hg"), qhg2)
                v.tensor_scalar(HgF[0:NH, 0:CCH // NH], HgF[0:NH, 0:CCH // NH],
                                crep[:, 0:1], None, ALU.mult)
                v.drain()
                for h in range(NH):
                    ins = v.scalar_tensor_tensor(
                        Hbuf[:, h * 64:h * 64 + 64],
                        Hbuf[:, h * 64:h * 64 + 64],
                        erep[:, h:h + 1],
                        HgF[0:NH, h * 64:h * 64 + 64], ALU.mult, ALU.add)
                ins.then_inc(*S.inc("hin_v"))
            V(v_comb)

    def emit_B2(Hbuf, Hbf, Schk, cumb, zraw, zs_tm, zsl, yzc,
                nrmc, sqp, nrf4, xdt_tm, xtm, bsc, csc, Xt, Mt, gmb,
                tcor, yv, yv2, deccol, drow, drep):
        hin_done = S.val("hin_v")
        zsil_done = {}
        vx31_done = {}

        for c in range(NCH):
            sl = slice(c * CHUNK, (c + 1) * CHUNK)
            base3 = PRE + c * CHUNK
            sc_defer = []

            prev_hb = S.val("b2_hb")

            inc_dec = S.inc("b2_s")

            def s_dec(s, c=c, sl=sl, inc=inc_dec, qdr=S.cnt["q_dr"]):
                if c == 0:
                    s.wait_ge(S.hdl("hin_v"), hin_done)
                s.wait_ge(S.hdl("q_dr"), qdr)
                s.activation(deccol[:], cum[:, sl.stop - 1:sl.stop], AF.Exp
                             ).then_inc(*inc)
            sc_defer.append(s_dec)
            sdec = S.val("b2_s")

            def y_pre(y, c=c, sdec=sdec, prev_hb=prev_hb):
                y.wait_ge(S.hdl("b2_hb"), prev_hb)
                DMA(y, "q_sd", Schk[:], s_dram[c])
                y.wait_ge(S.hdl("b2_s"), sdec)
                DMA(y, "q_dr", drow[:], deccol[:])
                y.wait_ge(S.hdl("q_dr"), S.val("q_dr"))
                DMA(y, "q_dr", drep[:], drow.unsqueeze(1).broadcast_to([1, NH, NH]))
                y.wait_ge(S.hdl("b2_s"), zsil_done.get(c - 1, 0))
                Z3 = z_dram.rearrange("(zt p) t -> p zt t", p=128)
                for zz in range(4):
                    DMA(y, "q_zr", zraw[:, zz * 8:(zz + 1) * 8, :], Z3[:, zz * 8:(zz + 1) * 8,
                                       base3:base3 + 128])
                    y.nop()
            SY(y_pre)
            q_sd_c = S.val("q_sd")
            q_zr_c = S.val("q_zr")
            q_dr_c = S.val("q_dr")

            _, wf3 = PF.acquire()

            def t_cum(t, c=c, sl=sl, wf3=wf3):
                t.wait_ge(S.hdl("pfe"), wf3)
                t.transpose(ptr_f[0:128, 0:64], cum[0:NH, sl],
                            eyef[0:NH, 0:NH]).then_inc(*S.inc("b2_t"))
            TEE(t_cum)
            tcv = S.val("b2_t")

            def v_ncum(v, tcv=tcv):
                v.wait_ge(S.hdl("b2_t"), tcv)
                v.tensor_scalar_mul(ctm[:, 0, :], ptr_f[0:128, 0:64], -1.0)
                v.drain().then_inc(*S.inc("b2_v"))
                v.drain().then_inc(*S.inc("pfe"))
            V(v_ncum)

            inc_xa = S.inc("b2_s")
            inc_xb = S.inc("pfe")

            def s_exp(s, tcv=tcv, inc_a=inc_xa, inc_b=inc_xb):
                s.wait_ge(S.hdl("b2_t"), tcv)
                s.activation(ctm[:, 1, :], ptr_f[0:128, 0:64], AF.Exp
                             ).then_inc(*inc_a)
                s.drain().then_inc(*inc_b)
            sc_defer.append(s_exp)
            PF.release(0)
            vnc = S.val("b2_v")
            sxc = S.val("b2_s")
            _, wf4 = PF.acquire()

            def t_dt2(t, sl=sl, vnc=vnc, sxc=sxc, wf4=wf4):
                t.wait_ge(S.hdl("b2_v"), vnc)
                t.wait_ge(S.hdl("b2_s"), sxc)
                t.wait_ge(S.hdl("pfe"), wf4)
                t.transpose(ptr_f[0:128, 0:64], dt_sb[0:NH, sl],
                            eyef[0:NH, 0:NH]).then_inc(*S.inc("b2_t"))
            TEE(t_dt2)
            tdt = S.val("b2_t")

            def v_dt2(v, tdt=tdt):
                v.wait_ge(S.hdl("b2_t"), tdt)
                v.tensor_copy(ctm[:, 2, :], ptr_f[0:128, 0:64])
                v.drain()
                v.tensor_copy(Hbf[:], Hbuf[:])
                v.drain().then_inc(*S.inc("b2_v"))
                v.drain().then_inc(*S.inc("pfe"))
            V(v_dt2)
            PF.release(0)
            vpre = S.val("b2_v")

            def y_cb(y, c=c, sl=sl, vx31=vx31_done.get(c - 1, 0)):
                if vx31:
                    y.wait_ge(S.hdl("b2_v"), vx31)
                DMA(y, "q_cb", cumrow_dram, cum[:, sl])
                y.wait_ge(S.hdl("q_cb"), S.val("q_cb"))
                DMA(y, "q_cb", cumb[:], cumrow_dram[0:1, 0:4096].unsqueeze(1)
                            .broadcast_to([1, 128, 4096]))
            SY(y_cb)
            q_cb_h0 = S.val("q_cb")

            for zq in range(8):
                inc_z = S.inc("b2_s")

                def s_z(s, zq=zq, q_zr_c=q_zr_c, inc=inc_z,
                        zgate=S.cnt["b2_zs"], tgate=S.cnt["b2_t"]):
                    if zq == 0:
                        s.wait_ge(S.hdl("q_zr"), q_zr_c)
                        s.wait_ge(S.hdl("b2_zs"), zgate)
                    else:
                        s.wait_ge(S.hdl("b2_t"), tgate)
                    s.activation(zsl.rearrange("p a b -> p (a b)"),
                                 zraw.rearrange("p a b -> p (a b)")
                                 [:, zq * 512:(zq + 1) * 512], AF.Silu
                                 ).then_inc(*inc)
                sc_defer.append(s_z)
                szt = S.val("b2_s")
                _, wlz = PB.acquire()

                def t_z(t, zq=zq, szt=szt, wlz=wlz):
                    t.wait_ge(S.hdl("b2_s"), szt)
                    t.wait_ge(S.hdl("pbe"), wlz)
                    for d in range(4):
                        ins = t.transpose(
                            ptr_b[0:128, d * 128:d * 128 + 128],
                            zsl[:, d, :], eye[:])
                    ins.then_inc(*S.inc("b2_t"))
                TEE(t_z)
                tzt = S.val("b2_t")

                def v_z(v, zq=zq, tzt=tzt):
                    v.wait_ge(S.hdl("b2_t"), tzt)
                    v.tensor_copy(zs_tm[:, zq * 4:zq * 4 + 4, :],
                                  ptr_b.ap().rearrange("p (a b) -> p a b",
                                                       a=4)
                                  ).then_inc(*S.inc("b2_zs"))
                    v.drain().then_inc(*S.inc("pbe"))
                V(v_z)
                PB.release(0)
            zsil_done[c] = S.val("b2_s")

            exp_done = {}
            gm_done = {}
            ty_done = {}
            vX_done = {}
            ye_done = {}

            def pair_stage1(P):
                h0 = 2 * P
                m = P // 2
                brow = (P // 2 % 2) * 64
                bt = 8 + m // 2
                ct = 16 + P
                par = P % 2
                q_cb_cur = S.val("q_cb")

                def v_X(v, c=c, sl=sl, P=P, h0=h0, q_cb_cur=q_cb_cur,
                        vpre=vpre, brow=brow, bt=bt, ct=ct, par=par,
                        xfree=exp_done.get(P - 2, 0),
                        cfree=ty_done.get(P - 2, 0)):
                    v.wait_ge(S.hdl("q_cb"), q_cb_cur)
                    v.wait_ge(S.hdl("b2_v"), vpre)
                    if xfree:
                        v.wait_ge(S.hdl("b2_s"), xfree)
                    for hi in range(2):
                        h = h0 + hi
                        hr = h % 32
                        ins = v.scalar_tensor_tensor(
                            Xt[:, par, hi, :],
                            cumb[:, hr * 128:hr * 128 + 128],
                            ctm[:, 0, h:h + 1], msk[:], ALU.add, ALU.add)
                    ins.then_inc(*S.inc("b2_v"))
                V(v_X)
                vX = S.val("b2_v")
                vX_done[P] = vX

                inc_M = S.inc("b2_s")

                def s_M(s, vX=vX, inc=inc_M, par=par, c=c, sl=sl,
                        brow=brow, bt=bt, ct=ct,
                        mfree=gm_done.get(P - 2, 0),
                        cfree=ty_done.get(P - 2, 0)):
                    s.wait_ge(S.hdl("b2_v"), vX)
                    if mfree:
                        s.wait_ge(S.hdl("b2_gm"), mfree)
                    if cfree:
                        s.wait_ge(S.hdl("b2_t"), cfree)
                    if brow == 64:
                        s.copy(bsc[:, par, :], xbc[64:128, bt, sl])
                    s.copy(csc[:, par, :], xbc[64:128, ct, sl])
                    s.activation(Mt[:, par, 0, :], Xt[:, par, 0, :], AF.Exp)
                    s.activation(Mt[:, par, 1, :], Xt[:, par, 1, :], AF.Exp
                                 ).then_inc(*inc)
                sc_defer.append(s_M)
                exp_done[P] = S.val("b2_s")

                def t_g(t, c=c, sl=sl, P=P, vX=vX, par=par,
                        brow=brow, bt=bt, ct=ct, ge_=(P % 2 == 0),
                        xr_s=exp_done[P],
                        gfree=gm_done.get(P - 2, 0)):
                    t.wait_ge(S.hdl("b2_s"), xr_s)
                    if gfree:
                        t.wait_ge(S.hdl("b2_gm"), gfree)
                    gb = pgg if ge_ else pss
                    Bop = bsc[:, par, :] if brow == 64 else xbc[0:64, bt, sl]
                    for hi in range(2):
                        Cop = csc[:, par, :] if hi == 1 else xbc[0:64, ct, sl]
                        ins = t.matmul(
                            gb[0:128, hi * 128:hi * 128 + 128],
                            Bop, Cop, start=True, stop=True)
                    ins.then_inc(*S.inc("b2_t"))
                TEE(t_g)
                return S.val("b2_t")

            def pair_stage2a(P, tg):
                h0 = 2 * P
                ct = 16 + P
                par = P % 2
                sM = exp_done[P]

                def v_gm(v, tg=tg, sM=sM, ge_=(P % 2 == 0), par=par,
                         gfree=ty_done.get(P - 2, 0)):
                    v.wait_ge(S.hdl("b2_t"), tg)
                    v.wait_ge(S.hdl("b2_s"), sM)
                    if gfree:
                        v.wait_ge(S.hdl("b2_t"), gfree)
                    gb = pgg if ge_ else pss
                    for hi in range(2):
                        ins = v.tensor_mul(gmb[:, par, hi, :],
                                           gb[0:128, hi * 128:hi * 128 + 128],
                                           Mt[:, par, hi, :])
                    ins.then_inc(*S.inc("b2_gm"))
                V(v_gm)
                gm_done[P] = S.val("b2_gm")
                vGM = S.val("b2_gm")

                ybk = pb[P % 2]
                cbk = pb[2 + P % 2]

                def t_y(t, P=P, h0=h0, vGM=vGM, ybk=ybk, cbk=cbk, c=c, sl=sl,
                        ct=ct, yfree=ye_done.get(P - 2, 0)):
                    t.wait_ge(S.hdl("b2_gm"), vGM)
                    if yfree:
                        t.wait_ge(S.hdl("b2_ye"), yfree)
                    par = P % 2
                    for hi in range(2):
                        Cop = csc[:, par, :] if hi == 1 else xbc[0:64, ct, sl]
                        t.matmul(ybk[0:128, hi * 64:hi * 64 + 64],
                                 gmb[:, par, hi, :],
                                 xdt_tm[:, (h0 + hi) % 8, :],
                                 start=True, stop=True)
                        ins = t.matmul(cbk[0:128, hi * 64:hi * 64 + 64],
                                       Cop,
                                       Hbf[:, (h0 + hi) * 64:(h0 + hi) * 64 + 64],
                                       start=True, stop=True)
                    ins.then_inc(*S.inc("b2_t"))
                TEE(t_y)
                ty_done[P] = S.val("b2_t")

            def pair_stage2b(P):
                h0 = 2 * P
                ybk = pb[P % 2]
                cbk = pb[2 + P % 2]
                ty = ty_done[P]

                def v_y(v, ty=ty, h0=h0, ybk=ybk, cbk=cbk, xr_v=S.val("b2_xt"),
                        c=c, P=P, qgbc=S.val("q_gbc")):
                    if c == 0 and P == 0:
                        v.wait_ge(S.hdl("q_gbc"), qgbc)
                    v.wait_ge(S.hdl("b2_t"), ty)
                    v.wait_ge(S.hdl("b2_xt"), xr_v)
                    for hi in range(2):
                        h = h0 + hi
                        xcol = ((h % 8) // 4) * 64
                        v.scalar_tensor_tensor(yv[:, hi * 64:hi * 64 + 64],
                                               xtm[:, xcol:xcol + 64],
                                               D_rep[:, h:h + 1],
                                               ybk[0:128, hi * 64:hi * 64 + 64],
                                               ALU.mult, ALU.add)
                    v.drain()
                    for hi in range(2):
                        h = h0 + hi
                        v.scalar_tensor_tensor(yv2[:, hi * 64:hi * 64 + 64],
                                               cbk[0:128, hi * 64:hi * 64 + 64],
                                               ctm[:, 1, h:h + 1],
                                               yv[:, hi * 64:hi * 64 + 64],
                                               ALU.mult, ALU.add)
                    v.drain()
                    v.wait_ge(S.hdl("b2_zs"), 8 * c + P // 4 + 1)
                    v.tensor_mul(yzc[:, h0 * 64:h0 * 64 + 128], yv2[:],
                                 zs_tm.rearrange("p a b -> p (a b)")
                                 [:, h0 * 64:h0 * 64 + 128]
                                 ).then_inc(*S.inc("b2_ye"))
                V(v_y)
                ye_done[P] = S.val("b2_ye")

            for xt in range(8):
                P0 = 4 * xt
                if P0 == 16:
                    def y_cb2(y, c=c, sl=sl, vx15=vX_done[15]):
                        y.wait_ge(S.hdl("b2_v"), vx15)
                        DMA(y, "q_cb", cumb[:], cumrow_dram[0:1, 4096:8192].unsqueeze(1)
                                    .broadcast_to([1, 128, 4096]))
                    SY(y_cb2)

                slx, wlx = PB.acquire()

                def t_x2(t, c=c, sl=sl, xt=xt, slx=slx, wlx=wlx,
                         tyf=ty_done.get(4 * xt - 1, 0),
                         vgate=S.cnt["b2_v"]):
                    t.wait_ge(S.hdl("b2_v"), vgate)
                    t.wait_ge(S.hdl("pbe"), wlx)
                    t.transpose(ptr_b[0:128, slx * 128:slx * 128 + 128],
                                xbc[:, xt, sl],
                                eye[:]).then_inc(*S.inc("b2_t"))
                TEE(t_x2)
                txt = S.val("b2_t")

                def v_x2(v, xt=xt, txt=txt, slx=slx,
                         tyf=ty_done.get(4 * xt - 1, 0)):
                    v.wait_ge(S.hdl("b2_t"), txt)
                    if tyf:
                        v.wait_ge(S.hdl("b2_t"), tyf)
                    v.tensor_copy(xtm[:], ptr_b[0:128, slx * 128:slx * 128 + 128])
                    for hh in range(8):
                        h = 8 * xt + hh
                        xcol = (hh // 4) * 64
                        v.tensor_scalar(xdt_tm[:, hh, :],
                                        ptr_b[0:128, slx * 128 + xcol:
                                              slx * 128 + xcol + 64],
                                        ctm[:, 2, h:h + 1], None,
                                        ALU.mult)
                    v.drain().then_inc(*S.inc("b2_xt"))
                    v.drain().then_inc(*S.inc("pbe"))
                V(v_x2)
                PB.release(slx)

                tg0 = pair_stage1(P0)
                tg1 = pair_stage1(P0 + 1)
                pair_stage2a(P0, tg0)
                tg2 = pair_stage1(P0 + 2)
                pair_stage2a(P0 + 1, tg1)
                pair_stage2b(P0)
                tg3 = pair_stage1(P0 + 3)
                pair_stage2a(P0 + 2, tg2)
                pair_stage2b(P0 + 1)
                pair_stage2a(P0 + 3, tg3)
                pair_stage2b(P0 + 2)
                pair_stage2b(P0 + 3)
            vx31_done[c] = vX_done[31]

            vyz = S.val("b2_ye")

            for sq in range(8):
                inc_q = S.inc("b2_s")

                def s_sq(s, sq=sq, vyz=vyz, inc=inc_q, vg=S.cnt["b2_v"]):
                    s.wait_ge(S.hdl("b2_ye"), vyz)
                    if sq > 1:
                        s.wait_ge(S.hdl("b2_v"), vg)
                    s.activation(sqp[:, sq % 2, :],
                                 yzc[:, sq * 512:(sq + 1) * 512],
                                 AF.Square).then_inc(*inc)
                sc_defer.append(s_sq)
                ssq = S.val("b2_s")

                def v_sq(v, sq=sq, ssq=ssq):
                    v.wait_ge(S.hdl("b2_s"), ssq)
                    v.tensor_reduce(ctm[:, 4, sq * 8:(sq + 1) * 8],
                                    sqp[:, sq % 2, :].rearrange(
                                        "p (h d) -> p h d", h=8),
                                    AX.X, ALU.add)
                    v.drain().then_inc(*S.inc("b2_v"))
                V(v_sq)
            vred = S.val("b2_v")

            def v_rs(v, vred=vred):
                v.wait_ge(S.hdl("b2_v"), vred)
                v.tensor_scalar(ctm[:, 4, :], ctm[:, 4, :], 1.0 / 64, EPS,
                                ALU.mult, ALU.add)
                v.drain()
                v.reciprocal(ctm[:, 5, :], ctm[:, 4, :])
                v.drain().then_inc(*S.inc("b2_v"))
            V(v_rs)
            vrr = S.val("b2_v")

            inc_rs = S.inc("b2_s")

            def s_rs(s, vrr=vrr, inc=inc_rs):
                s.wait_ge(S.hdl("b2_v"), vrr)
                s.activation(ctm[:, 5, :], ctm[:, 5, :], AF.Sqrt
                             ).then_inc(*inc)
            sc_defer.append(s_rs)
            srr = S.val("b2_s")

            def s_chunk(s, ops=list(sc_defer)):
                for f in ops:
                    f(s)
            SC(s_chunk)

            def v_nm(v, srr=srr):
                v.wait_ge(S.hdl("b2_s"), srr)
                for h in range(NH):
                    ins = v.scalar_tensor_tensor(
                        nrmc[:, h * 64:h * 64 + 64],
                        yzc[:, h * 64:h * 64 + 64],
                        ctm[:, 5, h:h + 1],
                        nw_bc[:, h * 64:h * 64 + 64], ALU.mult, ALU.mult)
                ins.then_inc(*S.inc("b2_nm"))
            V(v_nm)
            vnm = S.val("b2_nm")

            for fg in range(8):
                _, wnf = PB.acquire()

                def t_nf(t, fg=fg, vnm=vnm, wnf=wnf):
                    t.wait_ge(S.hdl("b2_nm"), vnm)
                    t.wait_ge(S.hdl("pbe"), wnf)
                    for d in range(4):
                        ft = fg * 4 + d
                        ins = t.transpose(
                            ptr_b[0:128, d * 128:d * 128 + 128],
                            nrmc[:, ft * 128:(ft + 1) * 128], eye[:])
                    ins.then_inc(*S.inc("b2_t"))
                TEE(t_nf)
                tnf = S.val("b2_t")

                def v_nf(v, fg=fg, tnf=tnf):
                    v.wait_ge(S.hdl("b2_t"), tnf)
                    v.wait_ge(S.hdl("q_nr"), max(0, S.cnt["q_nr"] - 16))
                    v.tensor_copy(nrf4[:, fg % 2, :, :],
                                  ptr_b.ap().rearrange("p (a b) -> p a b", a=4))
                    v.drain().then_inc(*S.inc("b2_nm"))
                    v.drain().then_inc(*S.inc("pbe"))
                V(v_nf)
                PB.release(0)
                vnf = S.val("b2_nm")

                def y_nf(y, c=c, fg=fg, vnf=vnf):
                    y.wait_ge(S.hdl("b2_nm"), vnf)
                    DMA(y, "q_nr",
                        nrm_dram.rearrange("(ft p) t -> p ft t", p=128)
                        [:, fg * 4:fg * 4 + 4, c * CHUNK:(c + 1) * CHUNK],
                        nrf4[:, fg % 2, :, :])
                SY(y_nf)

            def v_h(v, q_sd_c=q_sd_c, q_dr_c=q_dr_c):
                v.wait_ge(S.hdl("q_sd"), q_sd_c)
                v.wait_ge(S.hdl("q_dr"), q_dr_c)
                for h in range(NH):
                    ins = v.scalar_tensor_tensor(
                        Hbuf[:, h * 64:h * 64 + 64],
                        Hbuf[:, h * 64:h * 64 + 64],
                        drep[:, h:h + 1],
                        Schk[:, h * 64:h * 64 + 64], ALU.mult, ALU.add)
                ins.then_inc(*S.inc("b2_hb"))
            V(v_h)

        if "nrm" in debug_taps:
            tpn = tap("nrm", [DIN, T], BF16)
            qnr = S.val("q_nr")

            def g_tapn(g, qnr=qnr):
                g.wait_ge(S.hdl("q_nr"), qnr)
                DMA(g, "q_tap", tpn, nrm_dram)
            G(g_tapn)

    def emit_C(r2, nrh, wob, q1y, q1i, q1r, xq_tm, xqf, cstop=None):
        qnr_all = S.val("q_nr")
        qwob_all = S.val("q_wob")
        mm_done = {}
        kb_done = {}
        ev_done = {}
        def emit_c_mm(hf):
            def y_nrh(y, hf=hf, qnr_all=qnr_all, qwob_all=qwob_all,
                      prevmm=mm_done.get((0, 3), 0)):
                if hf == 0:
                    y.wait_ge(S.hdl("fin"), 1)
                    y.wait_ge(S.hdl("q_nr"), qnr_all)
                    y.wait_ge(S.hdl("q_wob"), qwob_all)
                else:
                    y.wait_ge(S.hdl("c_t"), prevmm)
                N3 = nrm_dram.rearrange("(ft p) t -> p ft t", p=128)
                for zz in range(4):
                    DMA(y, "q_nr", nrh[:, zz * 8:(zz + 1) * 8, :], N3[:, zz * 8:(zz + 1) * 8,
                                       hf * 512:hf * 512 + 512])
                    y.nop()
                for ts4 in range(4):
                    DMA(y, "q_ur", r2[:, hf * 4 + ts4, :],
                        u_in[PRE + hf * 512 + ts4 * 128:
                             PRE + hf * 512 + ts4 * 128 + 128, :])
            SY(y_nrh)
            qnr = S.val("q_nr")
            qur = S.val("q_ur")

            for db in range(4):
                for kb in range(4):
                    if kb >= 2:
                        tfree = kb_done.get((hf, db, kb - 2), 0)
                    else:
                        tfree = kb_done.get((hf, db - 1, kb + 2),
                                            kb_done.get((hf - 1, 3, kb + 2),
                                                        0))

                    qwoq = "q_wo" if (db * 4 + kb) % 2 == 0 else "q_wob2"

                    def y_wo(y, hf=hf, db=db, kb=kb, tfree=tfree, qwoq=qwoq):
                        W3b = wob_dram.rearrange("(kt p) d -> p kt d", p=128)
                        if tfree:
                            y.wait_ge(S.hdl("c_t"), tfree)
                        DMA(y, qwoq, wob[:, kb % 2, :, :],
                            W3b[:, kb * 8:(kb + 1) * 8,
                                db * 512:(db + 1) * 512])
                        y.nop()
                    SY(y_wo)
                    qwo = (qwoq, S.val(qwoq))

                    for tt4 in range(4):
                        def t_mm(t, hf=hf, db=db, kb=kb, tt4=tt4, qnr=qnr,
                                 qwo=qwo,
                                 evfree=ev_done.get((hf, db - 1, tt4), 0) if
                                 (hf, db - 1, tt4) in ev_done else
                                 ev_done.get((hf - 1, 3, tt4), 0)):
                            if kb == 0 and tt4 == 0:
                                t.wait_ge(S.hdl("q_nr"), qnr)
                            t.wait_ge(S.hdl(qwo[0]), qwo[1])
                            if kb == 0 and evfree:
                                t.wait_ge(S.hdl("c_v"), evfree)
                            bank = pb[tt4]
                            for j in range(8):
                                ins = t.matmul(
                                    bank[0:128, :],
                                    nrh[:, kb * 8 + j, tt4 * 128:tt4 * 128 + 128],
                                    wob[:, kb % 2, j, :],
                                    start=(kb == 0 and j == 0),
                                    stop=(kb == 3 and j == 7))
                            if tt4 == 3:
                                ins.then_inc(*S.inc("c_t"))
                        TEE(t_mm)
                    kb_done[(hf, db, kb)] = S.val("c_t")
                mm_done[(hf, db)] = S.val("c_t")
                tds = S.val("c_t")

                for tt4 in range(4):
                    def v_r1(v, hf=hf, db=db, tt4=tt4, tds=tds, qur=qur):
                        v.wait_ge(S.hdl("c_t"), tds)
                        v.wait_ge(S.hdl("q_ur"), qur)
                        v.tensor_add(
                            r2[:, hf * 4 + tt4, db * 512:(db + 1) * 512],
                            r2[:, hf * 4 + tt4, db * 512:(db + 1) * 512],
                            pb[tt4][0:128, :]
                            ).then_inc(*S.inc("c_v"))
                    V(v_r1)
                    ev_done[(hf, db, tt4)] = S.val("c_v")

            if cstop == "mm":
                return

        def emit_c_ln(hf):
            r1v = S.val("c_v")
            sc_ln1 = []
            xqev = {}
            for ts4 in range(4):
                tt = hf * 4 + ts4

                inc_sq = S.inc("c_s")
                sc_ln1.append(("sq", tt, r1v, S.val("c_v"), inc_sq))
                csq = S.val("c_s")

                def v_ln(v, tt=tt, csq=csq):
                    v.wait_ge(S.hdl("c_s"), csq)
                    v.reduce_sum(smallq[:, 0:1], q1y[:], AX.X)
                    v.drain()
                    v.reduce_sum(smallq[:, 1:2], r2[:, tt, :], AX.X)
                    v.drain()
                    v.tensor_scalar_mul(smallq[:, 2:3], smallq[:, 1:2], 1.0 / DM)
                    v.drain()
                    v.tensor_mul(smallq[:, 3:4], smallq[:, 2:3], smallq[:, 2:3])
                    v.drain()
                    v.scalar_tensor_tensor(smallq[:, 4:5], smallq[:, 0:1],
                                           1.0 / DM, smallq[:, 3:4],
                                           ALU.mult, ALU.subtract)
                    v.drain()
                    v.tensor_scalar_add(smallq[:, 4:5], smallq[:, 4:5], EPS)
                    v.drain()
                    v.reciprocal(smallq[:, 5:6], smallq[:, 4:5])
                    v.drain().then_inc(*S.inc("c_v"))
                V(v_ln)
                cln = S.val("c_v")
                inc_rt = S.inc("c_s")
                sc_ln1.append(("rt", tt, cln, 0, inc_rt))
                cs2 = S.val("c_s")

                def v_ap(v, tt=tt, cs2=cs2):
                    v.wait_ge(S.hdl("c_s"), cs2)
                    v.tensor_scalar(r2[:, tt, :], r2[:, tt, :], smallq[:, 2:3],
                                    smallq[:, 5:6], ALU.subtract, ALU.mult)
                    v.drain()
                    v.reduce_max(smallq[:, 6:7], r2[:, tt, :], AX.X,
                                 apply_absolute_value=True)
                    v.drain()
                    v.tensor_scalar(smallq[:, 6:7], smallq[:, 6:7], 1e-5, None,
                                    ALU.max)
                    v.drain()
                    v.reciprocal(smallq[:, 7:8], smallq[:, 6:7])
                    v.drain()
                    v.tensor_scalar_mul(smallq[:, 7:8], smallq[:, 7:8], 127.0)
                    v.drain()
                    v.tensor_scalar_mul(smallq[:, 8:9], smallq[:, 6:7],
                                        -1.0 / 127.0)
                    v.tensor_copy(smallq[:, 9 + tt:10 + tt], smallq[:, 6:7])
                    v.drain()
                    v.tensor_scalar_mul(smallq[:, 9 + tt:10 + tt],
                                        smallq[:, 9 + tt:10 + tt],
                                        -1.0 / 127.0)
                    v.drain()
                    v.tensor_scalar(q1y[:], r2[:, tt, :], smallq[:, 7:8], None,
                                    ALU.mult)
                    v.drain()
                    v.tensor_copy(q1i[:], q1y[:])
                    v.drain()
                    v.tensor_copy(q1r[:], q1i[:])
                    v.drain()
                    v.tensor_sub(q1y[:], q1y[:], q1r[:])
                    v.drain()
                    v.scalar_tensor_tensor(q1r[:], q1y[:], 0.5, q1r[:],
                                           ALU.is_gt, ALU.add)
                    v.drain()
                    v.scalar_tensor_tensor(xq_tm[:], q1y[:], -0.5, q1r[:],
                                           ALU.is_lt, ALU.subtract)
                    v.drain().then_inc(*S.inc("c_v"))
                V(v_ap)
                cqv = S.val("c_v")

                if cstop == "ln":
                    continue
                for qc in range(4):
                    bank = pgg if qc % 2 == 0 else pss

                    def t_xq(t, qc=qc, cqv=cqv, bank=bank,
                             ev=xqev.get(qc % 2, 0)):
                        t.wait_ge(S.hdl("c_v"), cqv)
                        if ev:
                            t.wait_ge(S.hdl("c_v"), ev)
                        for d in range(4):
                            kq = qc * 4 + d
                            ins = t.transpose(
                                bank[0:128, d * 128:d * 128 + 128],
                                xq_tm[:, kq * 128:(kq + 1) * 128], eyef[:])
                        ins.then_inc(*S.inc("c_t"))
                    TEE(t_xq)
                    txq = S.val("c_t")

                    def v_xq(v, hf=hf, ts4=ts4, qc=qc, txq=txq, bank=bank):
                        v.wait_ge(S.hdl("c_t"), txq)
                        v.tensor_copy(
                            xqf[:, hf, qc * 4:qc * 4 + 4,
                                ts4 * 128:ts4 * 128 + 128],
                            bank.ap().rearrange("p (a b) -> p a b", a=4))
                        v.drain().then_inc(*S.inc("c_v"))
                    V(v_xq)
                    xqev[qc % 2] = S.val("c_v")

            def s_ln1all(s, ops=list(sc_ln1)):
                for (kind, tt, g1, g2, inc_) in ops:
                    if kind == "sq":
                        s.wait_ge(S.hdl("c_v"), g1)
                        s.wait_ge(S.hdl("c_v"), g2)
                        s.activation(q1y[:], r2[:, tt, :], AF.Square
                                     ).then_inc(*inc_)
                    else:
                        s.wait_ge(S.hdl("c_v"), g1)
                        s.activation(smallq[:, 5:6], smallq[:, 5:6], AF.Sqrt
                                     ).then_inc(*inc_)
            SC(s_ln1all)


        emit_c_mm(0)
        if cstop == "mm":
            return
        emit_c_ln(0)
        emit_c_mm(1)
        emit_c_ln(1)
        if "r2" in debug_taps:
            tpr = tap("r2", [128, 8, DM])
            cvv = S.val("c_v")

            def g_tapr(g, cvv=cvv):
                g.wait_ge(S.hdl("c_v"), cvv)
                DMA(g, "q_tap", tpr, r2[:])
            G(g_tapr)

    def emit_D(r2, xqf):
        tb1_off = (AR.off + 31) & ~31
        tb1 = AR.alloc([128, 2, 16, 256], BF16)
        gpiece = AR.alloc([128, 2, 256], BF16)
        q2y_off = (AR.off + 31) & ~31
        q2y = AR.alloc([128, 4096], F32)
        q2i_off = (AR.off + 31) & ~31
        q2i = AR.alloc([128, 4096], I32)
        q2r = AR.alloc([128, 4096], F32)
        gld = AR.at(tb1_off + 8192, [128, 4096], BF16)
        q2rnd = AR.at(q2i_off, [128, 4096], F32)
        q2f = AR.at(q2y_off, [128, 32, 128], BF16)
        gmx = AR.alloc([128, 8, 32], F32)
        tb2 = AR.alloc([128, 2, 2, 8, 256], BF16)
        q2l = AR.alloc([128, 3, 8, 128], BF16)
        h2b = AR.alloc([128, 512], F32)
        outb = AR.at(tb1_off, [128, DM], F32)
        gmr = AR.alloc([128, 8, 2], F32)

        mm1_bank_free = {}
        gelu_done = {}
        reduce_done = {}
        piece_qgp = {}
        q2ev_done = {}
        mm2_done = {}
        sc_gelu = []

        for hb in range(32):
            qt1_prev = S.val("q_t1")

            def g_t1(g, hb=hb,
                     cfin=(S.val("c_t"), S.val("c_v"), S.val("c_s"))):
                if hb == 0:
                    g.wait_ge(S.hdl("c_t"), cfin[0])
                    g.wait_ge(S.hdl("c_v"), cfin[1])
                    g.wait_ge(S.hdl("c_s"), cfin[2])
                    g.wait_ge(S.hdl("cc2"), 1)
                if hb >= 2:
                    g.wait_ge(S.hdl("d_t"), mm1_bank_free.get(("t", hb - 2), 0))
                DMA(g, "q_t1",
                    tb1[:, hb % 2, :, :].rearrange("p a b -> p (a b)"),
                    T1V[hb // 4][:, hb % 4, :])
                g.nop()
            G(g_t1)
            qt1 = S.val("q_t1")

            for hf in range(2):
                for tt in range(4):
                    gi = hb * 8 + hf * 4 + tt

                    def t_m1(t, hf=hf, hb=hb, tt=tt, qt1=qt1, gi=gi):
                        t.wait_ge(S.hdl("q_t1"), qt1)
                        if gi == 0:
                            t.wait_ge(S.hdl("c_v"), S.cnt["c_v"])
                        if gi - 4 in mm1_bank_free:
                            t.wait_ge(S.hdl("d_s"), mm1_bank_free[gi - 4])
                        bank = pb[tt]
                        for k in range(16):
                            ins = t.matmul(bank[0:128, 0:256],
                                           xqf[:, hf, k, tt * 128:tt * 128 + 128],
                                           tb1[:, hb % 2, k, :],
                                           start=(k == 0), stop=(k == 15))
                        ins.then_inc(*S.inc("d_t"))
                    TEE(t_m1)
                    tm1 = S.val("d_t")

                    ginc = S.inc("d_s")
                    gelu_done[gi] = S.val("d_s")
                    mm1_bank_free[gi] = S.val("d_s")
                    sc_gelu.append((tm1,
                                    piece_qgp.get(gi - 2, 0),
                                    reduce_done.get(gi - 2, 0),
                                    hb, tt, hf, ginc))
                    sg = S.val("d_s")

                    def v_gmx(v, hf=hf, hb=hb, tt=tt, sg=sg, gi=gi):
                        v.wait_ge(S.hdl("d_s"), sg)
                        v.reduce_max(gmx[:, hf * 4 + tt, hb:hb + 1],
                                     gpiece[:, gi % 2, :], AX.X,
                                     apply_absolute_value=True)
                        v.drain().then_inc(*S.inc("d_v"))
                    V(v_gmx)
                    reduce_done[gi] = S.val("d_v")

                    def y_g(y, hf=hf, hb=hb, tt=tt, sg=sg, gi=gi):
                        y.wait_ge(S.hdl("d_s"), sg)
                        DMA(y, "q_gp", g_dram[hf * 512 + tt * 128:
                                               hf * 512 + tt * 128 + 128,
                                               hb * 256:(hb + 1) * 256], gpiece[:, gi % 2, :])
                    SY(y_g)
                    piece_qgp[gi] = S.val("q_gp")
            mm1_bank_free[("t", hb)] = S.val("d_t")

        def s_gelu(s, ops=list(sc_gelu)):
            for (tm1, qgp_g, rd_g, hb, tt, hf_, ginc) in ops:
                s.wait_ge(S.hdl("d_t"), tm1)
                if qgp_g:
                    s.wait_ge(S.hdl("q_gp"), qgp_g)
                if rd_g:
                    s.wait_ge(S.hdl("d_v"), rd_g)
                gi = hb * 8 + hf_ * 4 + tt
                s.activation(gpiece[:, gi % 2, :],
                             pb[tt][0:128, 0:256], AF.Gelu_apprx_tanh,
                             scale=smallq[:, 9 + hf_ * 4 + tt:
                                          10 + hf_ * 4 + tt]
                             ).then_inc(*ginc)
        SC(s_gelu)

        sc_q2_by_hf = {0: [], 1: []}
        qgp_all = S.val("q_gp")

        def emit_q2_tt(hf, tt):
            ttg = hf * 4 + tt
            gmv = S.val("d_v")

            def v_s2(v, ttg=ttg, tt=tt, hf=hf, qgp=qgp_all, gmv=gmv):
                if hf == 0 and tt == 0:
                    v.wait_ge(S.hdl("q_gp"), qgp)
                v.wait_ge(S.hdl("d_v"), gmv)
                v.reduce_max(smallq[:, 17:18], gmx[:, ttg, :], AX.X)
                v.drain()
                v.tensor_scalar(smallq[:, 17:18], smallq[:, 17:18], 1e-5,
                                None, ALU.max)
                v.drain()
                v.reciprocal(smallq[:, 18:19], smallq[:, 17:18])
                v.drain()
                v.tensor_scalar_mul(smallq[:, 18:19], smallq[:, 18:19],
                                    127.0)
                v.drain()
                v.tensor_scalar(gmr[:, ttg, 1:2], smallq[:, 17:18],
                                -1.0 / 127.0, None, ALU.mult)
                v.drain().then_inc(*S.inc("d_v"))
            V(v_s2)

            for sub in range(2):
                qgrq = "q_gr" if (ttg * 2 + sub) % 2 == 0 else "q_grb"

                def y_gl(y, ttg=ttg, sub=sub, dvv=S.val("d_v"), qgrq=qgrq):
                    y.wait_ge(S.hdl("d_v"), dvv)
                    DMA(y, qgrq, gld[:], g_dram[ttg * 128:ttg * 128 + 128,
                                           sub * 4096:(sub + 1) * 4096])
                SY(y_gl)
                qgr = (qgrq, S.val(qgrq))

                def v_q2(v, qgr=qgr, qq2c=S.val("q_q2")):
                    v.wait_ge(S.hdl(qgr[0]), qgr[1])
                    v.wait_ge(S.hdl("q_q2"), qq2c)
                    v.tensor_scalar(q2y[:], gld[:], smallq[:, 18:19], None,
                                    ALU.mult)
                    v.drain()
                    v.tensor_copy(q2i[:], q2y[:])
                    v.drain()
                    v.tensor_copy(q2r[:], q2i[:])
                    v.drain()
                    v.tensor_sub(q2y[:], q2y[:], q2r[:])
                    v.drain()
                    v.scalar_tensor_tensor(q2r[:], q2y[:], 0.5, q2r[:],
                                           ALU.is_gt, ALU.add)
                    v.drain()
                    v.scalar_tensor_tensor(q2rnd[:], q2y[:], -0.5, q2r[:],
                                           ALU.is_lt, ALU.subtract)
                    v.drain().then_inc(*S.inc("d_v"))
                V(v_q2)
                vq2 = S.val("d_v")

                for qq in range(8):
                    bank = pgg if qq % 2 == 0 else pss

                    def t_q2(t, qq=qq, vq2=vq2, bank=bank,
                             ev=q2ev_done.get(qq % 2, 0)):
                        t.wait_ge(S.hdl("d_v"), vq2)
                        if ev:
                            t.wait_ge(S.hdl("d_s"), ev)
                        for d in range(4):
                            kq = qq * 4 + d
                            ins = t.transpose(
                                bank[0:128, d * 128:d * 128 + 128],
                                q2rnd[:, kq * 128:(kq + 1) * 128],
                                eyef[:])
                        ins.then_inc(*S.inc("d_t"))
                    TEE(t_q2)
                    tq2 = S.val("d_t")
                    einc = S.inc("d_s")
                    q2ev_done[qq % 2] = S.val("d_s")
                    sc_q2_by_hf[hf].append((tq2, qq, bank,
                                            S.val("q_q2") if qq == 0 else 0,
                                            einc))
                sq2 = S.val("d_s")

                def y_q2(y, ttg=ttg, sub=sub, sq2=sq2):
                    y.wait_ge(S.hdl("d_s"), sq2)
                    DMA(y, "q_q2", q2_dram[ttg, sub],
                        q2f.rearrange("p a b -> p (a b)"))
                SY(y_q2)

        def emit_q2_flush(hf):
            def s_q2all(s, ops=list(sc_q2_by_hf[hf])):
                for (tq2, qq, bank, qq2p, einc) in ops:
                    s.wait_ge(S.hdl("d_t"), tq2)
                    if qq2p:
                        s.wait_ge(S.hdl("q_q2"), qq2p)
                    s.copy(q2f[:, qq * 4:qq * 4 + 4, :],
                           bank.ap().rearrange("p (a b) -> p a b", a=4)
                           ).then_inc(*einc)
            SC(s_q2all)

        def emit_mm2(hf, cb=None):
            qq2 = S.val("q_q2")
            for dt_ in range(4):
                for kg in range(8):
                    gi = dt_ * 8 + kg
                    ggi = hf * 32 + gi

                    qt2q = "q_t2" if ggi % 2 == 0 else "q_t2b"

                    def g_t2(g, dt_=dt_, kg=kg, ggi=ggi, qt2q=qt2q):
                        if ggi == 0:
                            g.wait_ge(S.hdl("cc2"), 1)
                        if ggi >= 2:
                            g.wait_ge(S.hdl("d_t"), mm2_done.get(ggi - 2, 0))
                        DMA(g, qt2q, tb2[:, ggi % 2, 0, :, :],
                            T2V[dt_ * 2][:, kg * 8:(kg + 1) * 8, :])
                        g.nop()
                        DMA(g, qt2q, tb2[:, ggi % 2, 1, :, :],
                            T2V[dt_ * 2 + 1][:, kg * 8:(kg + 1) * 8, :])
                        g.nop()
                    G(g_t2)
                    qt2 = (qt2q, S.val(qt2q))

                    for tt in range(4):
                        gt_ = ggi * 4 + tt

                        qlq = "q_q2l" if gt_ % 2 == 0 else "q_q2lb"

                        def y_l(y, hf=hf, kg=kg, tt=tt, gt_=gt_, qq2=qq2,
                                dt_=dt_, qlq=qlq):
                            if dt_ == 0 and kg == 0:
                                y.wait_ge(S.hdl("q_q2"), qq2)
                            if gt_ >= 3:
                                y.wait_ge(S.hdl("d_t"), mm2_done.get(("l", gt_ - 3), 0))
                            DMA(y, qlq,
                                q2l[:, gt_ % 3, :, :].rearrange(
                                    "p a b -> p (a b)"),
                                q2_dram[hf * 4 + tt, kg // 4]
                                [:, (kg % 4) * 1024:(kg % 4) * 1024 + 1024])
                        SY(y_l)
                        ql = (qlq, S.val(qlq))

                        def t_m2(t, dt_=dt_, kg=kg, tt=tt, qt2=qt2, ql=ql,
                                 ggi=ggi, gt_=gt_, hf=hf):
                            t.wait_ge(S.hdl(qt2[0]), qt2[1])
                            t.wait_ge(S.hdl(ql[0]), ql[1])
                            if kg == 0:
                                t.wait_ge(S.hdl("d_v"),
                                          mm2_done.get(("e", hf, dt_ - 1, tt), 0))
                            bank = pb[tt]
                            for k in range(8):
                                ins = t.matmul(bank[0:128, :],
                                               q2l[:, gt_ % 3, k, :],
                                               tb2[:, ggi % 2, :, k, :],
                                               start=(kg == 0 and k == 0),
                                               stop=(kg == 7 and k == 7))
                            ins.then_inc(*S.inc("d_t"))
                        TEE(t_m2)
                        mm2_done[("l", gt_)] = S.val("d_t")
                    mm2_done[ggi] = S.val("d_t")

                for tt in range(4):
                    tm2 = S.val("d_t")

                    def v_h2(v, hf=hf, dt_=dt_, tt=tt, tm2=tm2):
                        v.wait_ge(S.hdl("d_t"), tm2)
                        v.tensor_scalar(h2b[:], pb[tt][0:128, :],
                                        gmr[:, hf * 4 + tt, 1:2], None,
                                        ALU.mult)
                        v.drain()
                        v.tensor_add(r2[:, hf * 4 + tt, dt_ * 512:(dt_ + 1) * 512],
                                     r2[:, hf * 4 + tt, dt_ * 512:(dt_ + 1) * 512],
                                     h2b[:]).then_inc(*S.inc("d_v"))
                    V(v_h2)
                    mm2_done[("e", hf, dt_, tt)] = S.val("d_v")
                if cb is not None:
                    cb(dt_)

        def emit_ln(hf):
            rdone = S.val("d_v")
            qq2_gate = S.val("q_q2")
            sc_ln2 = []
            for tt in range(4):
                ttg = hf * 4 + tt

                inc_sq = S.inc("d_s")
                sq_gate = (rdone, S.val("d_v"))
                dsq = S.val("d_s")

                def v_ln2(v, ttg=ttg, dsq=dsq):
                    v.wait_ge(S.hdl("d_s"), dsq)
                    v.reduce_sum(smallq[:, 0:1], q2y[:, 0:2048], AX.X)
                    v.drain()
                    v.reduce_sum(smallq[:, 1:2], r2[:, ttg, :], AX.X)
                    v.drain()
                    v.tensor_scalar_mul(smallq[:, 2:3], smallq[:, 1:2], 1.0 / DM)
                    v.drain()
                    v.tensor_mul(smallq[:, 3:4], smallq[:, 2:3], smallq[:, 2:3])
                    v.drain()
                    v.scalar_tensor_tensor(smallq[:, 4:5], smallq[:, 0:1],
                                           1.0 / DM, smallq[:, 3:4],
                                           ALU.mult, ALU.subtract)
                    v.drain()
                    v.tensor_scalar_add(smallq[:, 4:5], smallq[:, 4:5], EPS)
                    v.drain()
                    v.reciprocal(smallq[:, 5:6], smallq[:, 4:5])
                    v.drain().then_inc(*S.inc("d_v"))
                V(v_ln2)
                dln = S.val("d_v")
                inc_rt = S.inc("d_s")
                sc_ln2.append((ttg, sq_gate, inc_sq, dln, inc_rt))
                dls = S.val("d_s")

                def v_out(v, ttg=ttg, dls=dls, qo=S.val("q_out")):
                    v.wait_ge(S.hdl("d_s"), dls)
                    v.wait_ge(S.hdl("q_out"), qo)
                    v.tensor_scalar(outb[:], r2[:, ttg, :], smallq[:, 2:3],
                                    smallq[:, 5:6], ALU.subtract, ALU.mult)
                    v.drain().then_inc(*S.inc("d_v"))
                V(v_out)
                dvo = S.val("d_v")

                def y_out(y, ttg=ttg, dvo=dvo):
                    y.wait_ge(S.hdl("d_v"), dvo)
                    DMA(y, "q_out", out[ttg * 128:(ttg + 1) * 128, :], outb[:])
                SY(y_out)

            def s_ln2all(s, ops=list(sc_ln2), qq2_gate=qq2_gate):
                s.wait_ge(S.hdl("q_q2"), qq2_gate)
                for (ttg, sq_gate, inc_sq, dln, inc_rt) in ops:
                    s.wait_ge(S.hdl("d_v"), sq_gate[0])
                    s.wait_ge(S.hdl("d_v"), sq_gate[1])
                    s.activation(q2y[:, 0:2048], r2[:, ttg, :], AF.Square
                                 ).then_inc(*inc_sq)
                    s.wait_ge(S.hdl("d_v"), dln)
                    s.activation(smallq[:, 5:6], smallq[:, 5:6], AF.Sqrt
                                 ).then_inc(*inc_rt)
            SC(s_ln2all)

        for tt in range(4):
            emit_q2_tt(0, tt)
        emit_q2_flush(0)

        def cb0(dt_):
            emit_q2_tt(1, dt_)
            if dt_ == 3:
                emit_q2_flush(1)
        emit_mm2(0, cb0)
        emit_ln(0)
        emit_mm2(1)
        emit_ln(1)

    with nc.Block() as block:
        holder["b"] = block
        emit(stop_after)

        qend = [(n, S.val(n)) for n in
                ["q_out", "q_tap", "q_tt", "q_z", "q_nr", "q_sd", "q_q2", "q_gp"]
                if S.val(n) > 0]

        def g_end(g):
            for (n, v) in qend:
                g.wait_ge(S.hdl(n), v)
        G(g_end)

    return nc, taps


def make_inputs(core, ins):
    s, b = divmod(core, 4)
    u = np.asarray(ins["L0_attn_u"][s], np.float32)
    t0 = b * T
    ue = np.zeros((TE_, DM), np.float32)
    if b == 0:
        ue[PRE:] = u[t0:t0 + T]
    else:
        ue[:] = u[t0 - PRE:t0 + T]
    selv = np.zeros((1, 4), np.float32)
    selv[0, :b] = 1.0
    return {
        "uT": np.ascontiguousarray(ue.T),
        "u": ue,
        "W_in": np.asarray(ins["L0_attn_W_in"], np.float32),
        "conv_w": np.ascontiguousarray(
            np.asarray(ins["L0_attn_conv_w"], np.float32)
            .reshape(48, 128, 4).transpose(1, 0, 2)),
        "conv_b": np.ascontiguousarray(
            np.asarray(ins["L0_attn_conv_b"], np.float32)
            .reshape(48, 128).T),
        "dt_bias": np.asarray(ins["L0_attn_dt_bias"], np.float32).reshape(-1, 1),
        "A_log": np.asarray(ins["L0_attn_A_log"], np.float32).reshape(-1, 1),
        "Dv": np.asarray(ins["L0_attn_D"], np.float32).reshape(1, -1),
        "nw": np.asarray(ins["L0_attn_norm_weight"], np.float32).reshape(1, -1),
        "W_out": np.asarray(ins["L0_attn_W_out"], np.float32),
        "w1": np.ascontiguousarray(
            np.asarray(ins["L0_ffn_w1"], np.float32)
            [core * (HID // 8):(core + 1) * (HID // 8)]),
        "w2": np.ascontiguousarray(
            np.asarray(ins["L0_ffn_w2"], np.float32)
            [core * (DM // 8):(core + 1) * (DM // 8)]),
        "sel4": selv,
    }


def kernel(**inputs):
    nc, _ = build()
    in_maps = [make_inputs(c, inputs) for c in range(8)]
    res = run_bass_kernel_spmd(nc, in_maps, core_ids=list(range(8)))
    outp = np.zeros((2, 4096, DM), np.float32)
    for c in range(8):
        s, b = divmod(c, 4)
        outp[s, b * T:(b + 1) * T] = res.results[c]["out"]
    return outp


# version 17
# speedup vs baseline: 1.0122x; 1.0122x over previous
import numpy as np
import ml_dtypes
import concourse.bass as bass
import concourse.mybir as mybir
from concourse.bass_utils import run_bass_kernel_spmd

F32 = mybir.dt.float32
BF16 = mybir.dt.bfloat16
I32 = mybir.dt.int32
U8 = mybir.dt.uint8
AF = mybir.ActivationFunctionType
ALU = mybir.AluOpType
AX = mybir.AxisListType
BF = ml_dtypes.bfloat16

DM = 2048
DIN = 4096
NH = 64
DS = 64
CHUNK = 128
EPS = 1e-5
CONV_DIM = DIN + 2 * 1024
DPROJ = 2 * DIN + 2 * 1024 + NH
HID = 8192

T = 1024
PRE = 3
TE_ = T + PRE
NCH = T // CHUNK
NEG = -1e30
DTSZ = {F32: 4, BF16: 2, I32: 4}


class Sems:
    def __init__(self, nc):
        self.nc = nc
        self.h = {}
        self.cnt = {}

    def new(self, *names):
        for name in names:
            self.h[name] = self.nc.alloc_semaphore(name=name)
            self.cnt[name] = 0

    def inc(self, name, by=1):
        self.cnt[name] += by
        return self.h[name], by

    def val(self, name):
        return self.cnt[name]

    def hdl(self, name):
        return self.h[name]


class Arena:
    def __init__(self, nc, name, nbytes):
        self.t = nc.alloc_sbuf_tensor(name, [128, nbytes], U8)
        self.n = nbytes
        self.off = 0
        self.stack = []

    def alloc(self, shape, dtype):
        assert shape[0] <= 128
        elems = 1
        for d in shape[1:]:
            elems *= d
        nb = elems * DTSZ[dtype]
        off = (self.off + 31) & ~31
        assert off + nb <= self.n, f"arena overflow: {off}+{nb} > {self.n}"
        self.off = off + nb
        ap = self.t.ap()[0:shape[0], off:off + nb].bitcast(dtype)
        if len(shape) > 2:
            pat = " ".join(f"a{i}" for i in range(len(shape) - 1))
            ap = ap.rearrange(f"p ({pat}) -> p {pat}",
                              **{f"a{i}": shape[i + 1] for i in range(len(shape) - 1)})
        return ap

    def at(self, off, shape, dtype):
        elems = 1
        for d in shape[1:]:
            elems *= d
        nb = elems * DTSZ[dtype]
        ap = self.t.ap()[0:shape[0], off:off + nb].bitcast(dtype)
        if len(shape) > 2:
            pat = " ".join(f"a{i}" for i in range(len(shape) - 1))
            ap = ap.rearrange(f"p ({pat}) -> p {pat}",
                              **{f"a{i}": shape[i + 1] for i in range(len(shape) - 1)})
        return ap

    def push(self):
        self.stack.append(self.off)

    def pop(self):
        self.off = self.stack.pop()


def build(stop_after=None, debug_taps=(), cstop=None):
    nc = bass.Bass("TRN2", target_bir_lowering=False)

    uT = nc.dram_tensor("uT", [DM, TE_], F32, kind="ExternalInput").ap()
    u_in = nc.dram_tensor("u", [TE_, DM], F32, kind="ExternalInput").ap()
    W_in = nc.dram_tensor("W_in", [DM, DPROJ], F32, kind="ExternalInput").ap()
    conv_w = nc.dram_tensor("conv_w", [128, 48, 4], F32, kind="ExternalInput").ap()
    conv_b = nc.dram_tensor("conv_b", [128, 48], F32, kind="ExternalInput").ap()
    dt_bias = nc.dram_tensor("dt_bias", [NH, 1], F32, kind="ExternalInput").ap()
    A_log = nc.dram_tensor("A_log", [NH, 1], F32, kind="ExternalInput").ap()
    Dv = nc.dram_tensor("Dv", [1, NH], F32, kind="ExternalInput").ap()
    nw_in = nc.dram_tensor("nw", [1, DIN], F32, kind="ExternalInput").ap()
    W_out = nc.dram_tensor("W_out", [DIN, DM], F32, kind="ExternalInput").ap()
    w1_in = nc.dram_tensor("w1", [HID // 8, DM], F32, kind="ExternalInput").ap()
    w2_in = nc.dram_tensor("w2", [DM // 8, HID], F32, kind="ExternalInput").ap()
    sel4 = nc.dram_tensor("sel4", [1, 4], F32, kind="ExternalInput").ap()

    out = nc.dram_tensor("out", [T, DM], F32, kind="ExternalOutput").ap()

    z_dram = nc.dram_tensor("z_dram", [DIN, TE_], BF16).ap()
    s_dram = nc.dram_tensor("s_dram", [NCH, NH, NH * DS], BF16).ap()
    nrm_dram = nc.dram_tensor("nrm_dram", [DIN, T], BF16).ap()
    TCC1 = DM * (HID // 8)
    TCC2 = HID * (DM // 8)
    TCC = TCC1 + TCC2
    tcc_in = nc.dram_tensor("tcc_in", [1, TCC], BF16).ap()
    tcc_out = nc.dram_tensor("tcc_out", [8, TCC], BF16,
                             addr_space="Shared").ap()
    T1V = [tcc_out[g:g + 1, 0:TCC1].rearrange(
        "g (p h4 x) -> (g p) h4 x", p=128, h4=4, x=4096)
        for g in range(8)]
    T2V = [tcc_out[g:g + 1, TCC1:TCC].rearrange(
        "g (p kt dl) -> (g p) kt dl", p=128, kt=64, dl=256)
        for g in range(8)]
    T1I = tcc_in[0:1, 0:TCC1].rearrange("g (p h4 x) -> (g p) h4 x",
                                        p=128, h4=4, x=4096)
    T2I = tcc_in[0:1, TCC1:TCC].rearrange("g (p kt dl) -> (g p) kt dl",
                                          p=128, kt=64, dl=256)
    g_dram = nc.dram_tensor("g_dram", [T, HID], BF16).ap()
    q2_dram = nc.dram_tensor("q2_dram", [T // 128, 2, 128, 4096], BF16).ap()
    cumrow_dram = nc.dram_tensor("cumrow_dram", [1, NH * CHUNK], F32).ap()
    wob_dram = nc.dram_tensor("wob_dram", [DIN, DM], BF16).ap()
    CCH = NH * NH * DS
    cc_in = nc.dram_tensor("cc_in", [1, CCH + NH], F32).ap()
    cc_out = nc.dram_tensor("cc_out", [4, CCH + NH], F32).ap()

    taps = {}

    def tap(name, shape, dtype=F32):
        taps[name] = nc.dram_tensor("tap_" + name, shape, dtype,
                                    kind="ExternalOutput").ap()
        return taps[name]

    sb = lambda name, shape, dtype=F32: nc.alloc_sbuf_tensor(name, shape, dtype)

    cstk = sb("cstk", [128, 8], F32)
    one_col = sb("one_col", [128, 1], F32)
    A_col = sb("A_col", [NH, 1], F32)
    dtb_col = sb("dtb_col", [NH, 1], F32)
    D_rep = sb("D_rep", [128, NH], F32)
    nw_bc = sb("nw_bc", [128, DIN], BF16)
    eye = sb("eye", [128, 128], BF16)
    eyef = sb("eyef", [128, 128], F32)
    msk = sb("msk", [128, 128], F32)
    zeros64 = sb("zeros64", [NH, CHUNK], F32)
    selc = sb("selc", [1, 4], F32)
    omc = sb("omc", [1, 4], F32)
    dt_sb = sb("dt_sb", [NH, T], F32)
    cum = sb("cum", [NH, T], F32)
    ctm = sb("ctm", [128, 6, NH], F32)
    smallq = sb("smallq", [128, 24], F32)
    xbc = None

    CONST = {1.0: 0, 0.5: 1, -0.5: 2, EPS: 3, 1.0 / 64: 4, 1e-5: 5, 127.0: 6,
             1.0 / DM: 7}

    def cc(v, p=128):
        i = CONST[v]
        return cstk[0:p, i:i + 1]

    rem = nc.sbuf_bytes_remaining
    per_part = rem // 128 if rem > 400000 else rem
    AR = Arena(nc, "arena", (per_part - 640) & ~63)

    pb = [nc.alloc_psum_tensor(f"pb{i}", [128, 512], F32) for i in range(4)]
    ptr_b = nc.alloc_psum_tensor("ptr_b", [128, 512], BF16)
    ptr_f = nc.alloc_psum_tensor("ptr_f", [128, 128], F32)
    pss = nc.alloc_psum_tensor("pss", [128, 512], F32)
    pgg = nc.alloc_psum_tensor("pgg", [128, 512], F32)

    S = Sems(nc)
    S.new("q_w", "q_u", "q_z", "q_misc", "q_tw", "q_tt", "q_sd", "q_cc",
          "q_cb", "q_nr", "q_wo", "q_ur", "q_t1", "q_gp", "q_q2", "q_t2",
          "q_q2l", "q_out", "q_zr", "q_dr", "q_hg", "q_gr", "q_gbc", "q_tap",
          "q_wob", "q_t2b", "q_q2lb", "q_wob2", "q_grb", "q_twb")
    S.new("g_set", "v_set", "s_set", "te_a", "ve_a", "cv", "cnv_i1", "cnv_v",
          "dt_v", "dt_s", "v_tern", "g_tern", "t_tern", "s_tern",
          "b1_s", "b1_v", "b1_t", "cc_done", "hin_v",
          "b2_s", "b2_v", "b2_t", "b2_gm", "b2_ye", "b2_zs", "b2_xt",
          "b2_hb", "b2_nm",
          "c_v", "c_s", "c_t", "d_v", "d_s", "d_t", "d_gm", "pbe", "pfe", "fin",
          "cc2")

    holder = {}

    class PsumRing:
        def __init__(self, sem, nslots):
            self.sem = sem
            self.n = nslots
            self.ev = [0] * nslots
            self.ctr = 0

        def acquire(self):
            sl_ = self.ctr % self.n
            self.ctr += 1
            return sl_, self.ev[sl_]

        def release(self, sl_):
            self.ev[sl_] = S.val(self.sem)

    PB = PsumRing("pbe", 1)
    PF = PsumRing("pfe", 1)

    def DMA(e, q, out, in_):
        if S.val(q):
            e.wait_ge(S.hdl(q), S.val(q))
        return e.dma_start(out=out, in_=in_).then_inc(*S.inc(q, 16))

    def G(fn): holder["b"].gpsimd(fn)
    def V(fn): holder["b"].vector(fn)
    def SC(fn): holder["b"].scalar(fn)
    def TEE(fn): holder["b"].tensor(fn)
    def SY(fn): holder["b"].sync(fn)

    JBS = []
    _j = DIN
    while _j < DPROJ:
        JBS.append((_j, min(512, DPROJ - _j)))
        _j += 512
    _j = 0
    while _j < DIN:
        JBS.append((_j, min(512, DIN - _j)))
        _j += 512

    def nsub(w):
        return (w + 127) // 128

    def ttiles_for(jg0):
        if DIN <= jg0 < DIN + CONV_DIM:
            return [(0, 512), (512, 512), (1024, PRE)]
        return [(PRE, 512), (PRE + 512, 512)]

    a_groups = []
    grp_upto_bi = {}
    for bi, (jb0, w) in enumerate(JBS):
        for si in range(nsub(w)):
            jw = min(128, w - si * 128)
            for (t0, tn) in ttiles_for(jb0 + si * 128):
                a_groups.append((bi, jb0 + si * 128, jw, t0, tn))
        grp_upto_bi[bi] = len(a_groups)

    def emit(stop_after):
        nonlocal xbc
        AR.push()
        xbc = AR.alloc([128, 48, TE_], BF16)
        AR.push()
        uT_sb = AR.alloc([128, 16, TE_], BF16)
        wbuf = AR.alloc([128, 2, 16, 512], BF16)
        zb = AR.alloc([128, 2, 512], BF16)
        dtraw = AR.alloc([NH, TE_], F32)
        dts = AR.alloc([NH, 2, T], F32)
        convw_sb = AR.alloc([128, 48, 4], F32)
        convb_sb = AR.alloc([128, 48], F32)
        acc = AR.alloc([128, 2, T], F32)

        def g_fn(g):
            for v, i in CONST.items():
                g.memset(cstk[:, i:i + 1], float(v))
            g.memset(one_col[:], 1.0)
            g.memset(zeros64[:], 0.0)
            g.memset(eye[:], 0.0)
            g.memset(msk[:], 0.0)
            g.drain()
            g.affine_select(out=eye[:], in_=eye[:], compare_op=ALU.not_equal,
                            fill=1.0, base=0, pattern=[[-1, 128]],
                            channel_multiplier=1)
            g.affine_select(out=msk[:], in_=msk[:], compare_op=ALU.is_ge,
                            fill=NEG, base=0, pattern=[[1, 128]],
                            channel_multiplier=-1)
            g.drain().then_inc(*S.inc("g_set"))
            uT3 = uT.rearrange("(kt p) t -> p kt t", p=128)
            DMA(g, "q_u", uT_sb[:, 0:8, :], uT3[:, 0:8, :])
            g.nop()
            DMA(g, "q_u", uT_sb[:, 8:16, :], uT3[:, 8:16, :])
            g.nop()
            DMA(g, "q_gbc", nw_bc[:], nw_in.unsqueeze(1).broadcast_to([1, 128, DIN]))
            DMA(g, "q_gbc", D_rep[:], Dv.unsqueeze(1).broadcast_to([1, 128, NH]))
        G(g_fn)

        def y_fn(y):
            DMA(y, "q_misc", dtb_col[:], dt_bias)
            DMA(y, "q_misc", A_col[:], A_log)
            DMA(y, "q_misc", selc[:], sel4)
            DMA(y, "q_misc", convw_sb[:], conv_w)
            DMA(y, "q_misc", convb_sb[:], conv_b)
        SY(y_fn)
        misc = S.val("q_misc")

        def s_fn(s):
            s.wait_ge(S.hdl("q_misc"), misc)
            s.activation(A_col[:], A_col[:], AF.Exp)
            s.drain().then_inc(*S.inc("s_set"))
        SC(s_fn)
        gv, sv = S.val("g_set"), S.val("s_set")

        def v_fn(v):
            v.wait_ge(S.hdl("g_set"), gv)
            v.tensor_copy(eyef[:], eye[:])
            v.wait_ge(S.hdl("s_set"), sv)
            v.tensor_scalar_mul(A_col[:], A_col[:], -1.0)
            v.wait_ge(S.hdl("q_misc"), misc)
            v.tensor_scalar(omc[0:1, :], selc[:], -1.0, 1.0, ALU.mult, ALU.add)
            v.drain().then_inc(*S.inc("v_set"))
        V(v_fn)
        vset = S.val("v_set")

        def g_fn2(g):
            W3 = W_in.rearrange("(kt p) j -> p kt j", p=128)
            for bi, (jb0, w) in enumerate(JBS):
                if bi >= 2:
                    g.wait_ge(S.hdl("te_a"), grp_upto_bi[bi - 2])
                DMA(g, "q_w", wbuf[:, bi % 2, 0:8, 0:w], W3[:, 0:8, jb0:jb0 + w])
                g.nop()
                DMA(g, "q_w", wbuf[:, bi % 2, 8:16, 0:w], W3[:, 8:16, jb0:jb0 + w])
                g.nop()
        G(g_fn2)

        def t_fn(t):
            t.wait_ge(S.hdl("q_u"), 32)
            t.wait_ge(S.hdl("v_set"), vset)
            for idx, (bi, jg0, jw, t0, tn) in enumerate(a_groups):
                if idx == 0 or a_groups[idx - 1][0] != bi:
                    t.wait_ge(S.hdl("q_w"), 32 * (bi + 1))
                if idx >= 4:
                    t.wait_ge(S.hdl("ve_a"), idx - 3)
                bank = pb[idx % 4]
                jl0 = jg0 - JBS[bi][0]
                for k in range(16):
                    ins = t.matmul(bank[0:jw, 0:tn],
                                   wbuf[:, bi % 2, k, jl0:jl0 + jw],
                                   uT_sb[:, k, t0:t0 + tn],
                                   start=(k == 0), stop=(k == 15))
                ins.then_inc(*S.inc("te_a"))
        TEE(t_fn)

        zdma = []
        nz_total = sum(1 for (bi, jg0, jw, t0, tn) in a_groups if jg0 < DIN)

        def v_fn2(v):
            nz = 0
            for idx, (bi, jg0, jw, t0, tn) in enumerate(a_groups):
                v.wait_ge(S.hdl("te_a"), idx + 1)
                bank = pb[idx % 4]
                if jg0 < DIN:
                    if nz >= 2:
                        v.wait_ge(S.hdl("q_z"), 16 * (nz - 1))
                    v.tensor_copy(zb[0:jw, nz % 2, 0:tn], bank[0:jw, 0:tn])
                    v.drain().then_inc(*S.inc("ve_a"))
                    zdma.append((S.val("ve_a"), nz % 2, jg0, jw, t0, tn))
                    nz += 1
                elif jg0 < DIN + CONV_DIM:
                    ct = (jg0 - DIN) // 128
                    v.tensor_copy(xbc[0:jw, ct, t0:t0 + tn], bank[0:jw, 0:tn])
                    v.drain().then_inc(*S.inc("ve_a"))
                else:
                    v.tensor_copy(dtraw[0:jw, t0:t0 + tn], bank[0:jw, 0:tn])
                    v.drain().then_inc(*S.inc("ve_a"))
        V(v_fn2)

        def y_fn2(y):
            for (vev, par, jg0, jw, t0, tn) in zdma:
                y.wait_ge(S.hdl("ve_a"), vev)
                DMA(y, "q_z", z_dram[jg0:jg0 + jw, t0:t0 + tn], zb[0:jw, par, 0:tn])
        SY(y_fn2)

        ct_ready = {}
        dt_ready = 0
        for idx, (bi, jg0, jw, t0, tn) in enumerate(a_groups):
            if DIN <= jg0 < DIN + CONV_DIM:
                ct_ready[(jg0 - DIN) // 128] = idx + 1
            if jg0 >= DIN + CONV_DIM:
                dt_ready = max(dt_ready, idx + 1)

        def s_fn2(s):
            s.wait_ge(S.hdl("q_misc"), misc)
            for ct in range(48):
                s.wait_ge(S.hdl("ve_a"), ct_ready[ct])
                if ct >= 2:
                    s.wait_ge(S.hdl("cnv_v"), ct - 1)
                s.activation(acc[:, ct % 2, :], xbc[:, ct, 0:T], AF.Identity,
                             bias=convb_sb[:, ct:ct + 1],
                             scale=convw_sb[:, ct, 0:1])
                s.drain().then_inc(*S.inc("cnv_i1"))
                s.wait_ge(S.hdl("cnv_v"), ct + 1)
                s.activation(xbc[:, ct, 0:T], acc[:, ct % 2, :], AF.Silu)
                s.drain().then_inc(*S.inc("cv"))
        SC(s_fn2)

        def v_fn3(v):
            for ct in range(48):
                v.wait_ge(S.hdl("cnv_i1"), ct + 1)
                for k in (1, 2, 3):
                    v.scalar_tensor_tensor(acc[:, ct % 2, :],
                                           xbc[:, ct, k:k + T],
                                           convw_sb[:, ct, k:k + 1],
                                           acc[:, ct % 2, :], ALU.mult, ALU.add)
                    v.drain()
                v.nop().then_inc(*S.inc("cnv_v"))
        V(v_fn3)

        def v_fn4(v):
            v.wait_ge(S.hdl("ve_a"), dt_ready)
            v.tensor_scalar(dt_sb[:], dtraw[:, PRE:PRE + T], dtb_col[:], None,
                            ALU.add)
            v.drain().then_inc(*S.inc("dt_v"))
            v.wait_ge(S.hdl("dt_s"), 1)
            v.tensor_scalar(dts[:, 0, :], dt_sb[:], 0.0, None, ALU.max)
            v.drain()
            v.tensor_add(dt_sb[:], dts[:, 0, :], dts[:, 1, :])
            v.drain()
            v.tensor_scalar(dts[:, 0, :], dt_sb[:], A_col[:], None, ALU.mult)
            v.drain()
            for c in range(NCH):
                v.tensor_tensor_scan(cum[:, c * CHUNK:(c + 1) * CHUNK],
                                     dts[:, 0, c * CHUNK:(c + 1) * CHUNK],
                                     zeros64[:], 0.0, ALU.add, ALU.add)
            v.drain().then_inc(*S.inc("dt_v"))
        V(v_fn4)

        def s_fn3(s):
            s.wait_ge(S.hdl("dt_v"), 1)
            s.activation(dts[:, 0, :], dt_sb[:], AF.Abs)
            s.drain()
            s.activation(dts[:, 1, :], dts[:, 0, :], AF.Exp, scale=-1.0)
            s.drain()
            s.activation(dts[:, 1, :], dts[:, 1, :], AF.Ln,
                         bias=one_col[0:NH, :], scale=1.0)
            s.drain().then_inc(*S.inc("dt_s"))
        SC(s_fn3)

        if "xs" in debug_taps:
            tp1 = tap("xs", [128, 48, T], BF16)
            tp2 = tap("dt", [NH, T])
            tp3 = tap("cum", [NH, T])
            cvv, dvv = 48, 2

            def g_tap(g):
                g.wait_ge(S.hdl("cv"), cvv)
                g.wait_ge(S.hdl("dt_v"), dvv)
                DMA(g, "q_tap", tp1, xbc[:, :, 0:T])
                DMA(g, "q_tap", tp2, dt_sb[:])
                DMA(g, "q_tap", tp3, cum[:])
            G(g_tap)

        AR.pop()

        AR.push()
        twk = AR.alloc([128, 4, 2048], F32)
        sga = AR.alloc([128, 2, 2048], BF16)
        sgb = AR.alloc([128, 2, 2048], BF16)
        ttn = AR.alloc([128, 2, 2048], BF16)
        tdg = AR.alloc([128, 2, 128], BF16)
        tsc = AR.alloc([128, 8], F32)
        tev1 = AR.alloc([128, 16, 256], BF16)
        tev2 = AR.alloc([128, 64, 128], BF16)

        tern_groups = [("w1", [u]) for u in range(8)] + \
                      [("w2", [8, 9, 10, 11]), ("w2", [12, 13, 14, 15])]
        sign_done = {}
        add_done = {}
        te_done = {}
        evac_done = {}
        glast_te = {}
        store_qtt = {}

        for gi, (kind, units) in enumerate(tern_groups):
            DW = DM if kind == "w1" else HID
            gpar = gi % 2
            for u in units:
                qtwq = "q_tw" if u % 2 == 0 else "q_twb"

                def g_ld(g, u=u, kind=kind, tea=S.val("te_a"), qtwq=qtwq,
                         qz=S.val("q_z"),
                         free=(sign_done.get(u - 4, 0))):
                    if u == 0:
                        g.wait_ge(S.hdl("te_a"), tea)
                        g.wait_ge(S.hdl("q_z"), qz)
                        g.wait_ge(S.hdl("dt_v"), 2)
                        g.wait_ge(S.hdl("cv"), 48)
                    if u >= 4:
                        g.wait_ge(S.hdl("s_tern"), free)
                    if kind == "w1":
                        src_ap = w1_in[u * 128:u * 128 + 128, :]
                    else:
                        gg, j = (u - 8) // 4, (u - 8) % 4
                        src_ap = w2_in[gg * 128:gg * 128 + 128,
                                       j * 2048:(j + 1) * 2048]
                    DMA(g, qtwq, twk[:, u % 4, :], src_ap)
                G(g_ld)
            qtw = (S.val("q_tw"), S.val("q_twb"))

            def v_red(v, kind=kind, units=units, qtw=qtw, DW=DW, gpar=gpar,
                      tprev=glast_te.get(gi - 2, 0)):
                v.wait_ge(S.hdl("q_tw"), qtw[0])
                if qtw[1]:
                    v.wait_ge(S.hdl("q_twb"), qtw[1])
                for i, u in enumerate(units):
                    col = 0 if kind == "w1" else 4 + i
                    v.reduce_sum(tsc[:, col:col + 1], twk[:, u % 4, :], AX.X,
                                 apply_absolute_value=True)
                v.drain()
                if kind == "w2":
                    v.tensor_add(tsc[:, 0:1], tsc[:, 4:5], tsc[:, 5:6])
                    v.tensor_add(tsc[:, 1:2], tsc[:, 6:7], tsc[:, 7:8])
                    v.drain()
                    v.tensor_add(tsc[:, 0:1], tsc[:, 0:1], tsc[:, 1:2])
                    v.drain()
                v.tensor_scalar(tsc[:, 1:2], tsc[:, 0:1], 1.0 / DW, 1e-5,
                                ALU.mult, ALU.max)
                v.drain()
                v.reciprocal(tsc[:, 2:3], tsc[:, 1:2])
                v.tensor_scalar(tsc[:, 3:4], tsc[:, 1:2], 0.5, None, ALU.mult)
                v.drain()
                v.wait_ge(S.hdl("t_tern"), tprev)
                v.tensor_scalar(tdg[:, gpar, :], eye[:], tsc[:, 3:4], None,
                                ALU.mult)
                v.drain().then_inc(*S.inc("v_tern"))
            V(v_red)
            scale_done = S.val("v_tern")

            for u in units:
                par = u % 2

                def s_sg(s, u=u, par=par, scale_done=scale_done,
                         abfree=add_done.get(u - 2, 0)):
                    s.wait_ge(S.hdl("v_tern"), scale_done)
                    if u >= 2:
                        s.wait_ge(S.hdl("v_tern"), abfree)
                    s.activation(sga[:, par, :], twk[:, u % 4, :], AF.Sign,
                                 bias=cc(-0.5), scale=tsc[:, 2:3])
                    ins = s.activation(sgb[:, par, :], twk[:, u % 4, :],
                                       AF.Sign, bias=cc(0.5),
                                       scale=tsc[:, 2:3])
                    ins.then_inc(*S.inc("s_tern"))
                SC(s_sg)
                sign_done[u] = S.val("s_tern")

                def v_add(v, u=u, par=par, sd=sign_done[u],
                          tfree=te_done.get((u - 2, 3), 0)):
                    v.wait_ge(S.hdl("s_tern"), sd)
                    if u >= 2:
                        v.wait_ge(S.hdl("t_tern"), tfree)
                    v.tensor_add(ttn[:, par, :], sga[:, par, :],
                                 sgb[:, par, :])
                    v.drain().then_inc(*S.inc("v_tern"))
                V(v_add)
                add_done[u] = S.val("v_tern")

                for q in range(4):
                    gq = u * 4 + q
                    bank = pgg if gq % 2 == 0 else pss

                    def t_tr(t, u=u, q=q, par=par, gpar=gpar, bank=bank,
                             ad=add_done[u], ev=evac_done.get(gq - 2, 0)):
                        t.wait_ge(S.hdl("v_tern"), max(ad, ev))
                        for d in range(4):
                            ins = t.matmul(
                                bank[0:128, d * 128:d * 128 + 128],
                                ttn[:, par, (q * 4 + d) * 128:
                                    (q * 4 + d) * 128 + 128],
                                tdg[:, gpar, :], start=True, stop=True)
                        ins.then_inc(*S.inc("t_tern"))
                    TEE(t_tr)
                    te_done[(u, q)] = S.val("t_tern")

                    if u < 8:
                        stgate = store_qtt.get((u - 2) // 2, 0) \
                            if u >= 2 else 0
                    else:
                        stgate = store_qtt.get(("t2", 0), 0) \
                            if u >= 12 else 0

                    def v_ev(v, u=u, q=q, bank=bank, td=te_done[(u, q)],
                             stgate=stgate):
                        v.wait_ge(S.hdl("t_tern"), td)
                        if stgate:
                            v.wait_ge(S.hdl("q_tt"), stgate)
                        if u < 8:
                            dest = tev1[:, q * 4:q * 4 + 4,
                                        (u % 2) * 128:(u % 2) * 128 + 128]
                        else:
                            j = (u - 8) % 4
                            dest = tev2[:, j * 16 + q * 4:j * 16 + q * 4 + 4,
                                        :]
                        v.tensor_copy(dest,
                                      bank.ap().rearrange("p (a b) -> p a b",
                                                          a=4))
                        v.drain().then_inc(*S.inc("v_tern"))
                    V(v_ev)
                    evac_done[gq] = S.val("v_tern")
            glast_te[gi] = S.val("t_tern")

            if kind == "w1" and units[0] % 2 == 1:
                pair = units[0] // 2
                evs = S.val("v_tern")

                def y_st1(y, pair=pair, evs=evs):
                    y.wait_ge(S.hdl("v_tern"), evs)
                    DMA(y, "q_tt", T1I[:, pair, :],
                        tev1.rearrange("p a b -> p (a b)"))
                    y.nop()
                SY(y_st1)
                store_qtt[pair] = S.val("q_tt")
            if kind == "w2":
                gg = 0 if units[0] == 8 else 1
                evs = S.val("v_tern")

                def y_st2(y, gg=gg, evs=evs):
                    y.wait_ge(S.hdl("v_tern"), evs)
                    for b in range(8):
                        DMA(y, "q_tt",
                            T2I[:, b * 8:b * 8 + 8,
                                gg * 128:gg * 128 + 128],
                            tev2[:, b * 8:b * 8 + 8, :])
                        y.nop()
                SY(y_st2)
                store_qtt[("t2", gg)] = S.val("q_tt")

        if "tern" in debug_taps:
            tpt = tap("tern", [1, TCC], BF16)
            qtt = S.val("q_tt")

            def g_tapt(g, qtt=qtt):
                g.wait_ge(S.hdl("q_tt"), qtt)
                DMA(g, "q_tap", tpt, tcc_in)
            G(g_tapt)
        AR.pop()

        if stop_after == "A":
            return

        AR.push()
        Hbuf = AR.alloc([NH, NH * DS], F32)
        Hbf = AR.alloc([NH, NH * DS], BF16)
        Schk = AR.alloc([NH, NH * DS], BF16)
        cumb = AR.alloc([128, 32 * CHUNK], F32)
        zraw = AR.alloc([128, 32, 128], BF16)
        zs_tm = AR.alloc([128, 32, 128], BF16)
        yzc = AR.alloc([128, NH * DS], BF16)
        nrmc = yzc
        sqp = AR.alloc([128, 2, 512], F32)
        nrf4 = AR.alloc([128, 2, 4, 128], BF16)
        xdt_tm = AR.alloc([128, 8, 64], BF16)
        bdec_tm = AR.alloc([128, 8, 64], BF16)
        xtm = AR.alloc([128, 128], BF16)
        bsc = AR.alloc([64, 2, 128], BF16)
        csc = AR.alloc([64, 2, 128], BF16)
        Xt = AR.alloc([128, 2, 2, 128], F32)
        Mt = AR.alloc([128, 2, 2, 128], F32)
        gmb = AR.alloc([128, 2, 2, 128], BF16)
        tcor = None
        yv = AR.alloc([128, 128], F32)
        yv2 = AR.alloc([128, 128], F32)
        dcyf = AR.alloc([NH, CHUNK], F32)
        deccol = AR.alloc([NH, 1], F32)
        dagg = AR.alloc([NH, 1], F32)
        drow = AR.alloc([1, NH], F32)
        drep = AR.alloc([NH, NH], F32)
        row64 = AR.alloc([1, NH], F32)
        erep = AR.alloc([NH, NH], F32)
        crep = AR.alloc([NH, 1], F32)
        zsl = AR.alloc([128, 4, 128], BF16)

        emit_B1(Hbuf, Schk, dcyf, deccol, dagg, drow, drep, xdt_tm, bdec_tm)

        def g_cc2(g, qtt=S.val("q_tt")):
            g.wait_ge(S.hdl("q_tt"), qtt)
            g.collective_compute(
                "AllGather", ALU.bypass, ins=[tcc_in], outs=[tcc_out],
                replica_groups=[[0, 1, 2, 3, 4, 5, 6, 7]],
            ).then_inc(*S.inc("cc2"))
        G(g_cc2)

        wcast = AR.alloc([128, 1, DM // 4], BF16)

        def g_wcast(g):
            W3 = W_out.rearrange("(kt p) d -> p kt d", p=128)
            W3b = wob_dram.rearrange("(kt p) d -> p kt d", p=128)
            for i in range(128):
                kt, half = i // 4, (i % 4) * (DM // 4)
                DMA(g, "q_wob", wcast[:, :, :],
                    W3[:, kt:kt + 1, half:half + DM // 4])
                g.nop()
                DMA(g, "q_wob", W3b[:, kt:kt + 1, half:half + DM // 4],
                    wcast[:, :, :])
                g.nop()
        G(g_wcast)

        emit_hin(Hbuf, cumb, row64, erep, crep)
        if "hin" in debug_taps:
            tph = tap("hin", [NH, NH * DS])
            hv = S.val("hin_v")

            def g_taph(g, hv=hv):
                g.wait_ge(S.hdl("hin_v"), hv)
                DMA(g, "q_tap", tph, Hbuf[:])
            G(g_taph)
        if stop_after != "B1":
            emit_B2(Hbuf, Hbf, Schk, cumb, zraw, zs_tm, zsl, yzc,
                    nrmc, sqp, nrf4, xdt_tm, xtm, bsc, csc, Xt, Mt, gmb,
                    tcor, yv, yv2, deccol, drow, drep)
        def g_b2fin(g, waits=[(n, S.val(n)) for n in
                              ("b2_hb", "b2_ye", "b2_nm", "b2_zs", "b2_s",
                               "b2_t", "b2_v", "b2_gm", "b2_xt", "q_nr",
                               "q_sd", "q_dr", "q_cb", "q_zr")]):
            for (n, v) in waits:
                if v:
                    g.wait_ge(S.hdl(n), v)
            g.drain().then_inc(*S.inc("fin"))
        G(g_b2fin)
        AR.pop()
        AR.pop()
        if stop_after in ("B1", "B2"):
            return

        AR.push()
        r2 = AR.alloc([128, 8, DM], F32)
        xqf = AR.alloc([128, 2, 16, 512], BF16)
        AR.push()
        nrh = AR.alloc([128, 32, 512], BF16)
        q1y = AR.alloc([128, DM], F32)
        q1i = AR.alloc([128, DM], I32)
        q1r = AR.alloc([128, DM], F32)
        wob = AR.alloc([128, 2, 8, 512], BF16)
        xq_tm = AR.alloc([128, DM], F32)
        emit_C(r2, nrh, wob, q1y, q1i, q1r, xq_tm, xqf, cstop)
        AR.pop()
        if stop_after == "C":
            return
        AR.push()
        emit_D(r2, xqf)
        AR.pop()
        AR.pop()

    def emit_B1(Hbuf, Schk, dcyf, deccol, dagg, drow, drep, xdt_tm, bdec_tm):
        def g_init(g, vt=S.val("v_tern"), tt=S.val("t_tern"),
                   st_=S.val("s_tern"), qtt=S.val("q_tt"),
                   qz=S.val("q_z")):
            g.wait_ge(S.hdl("cv"), 48)
            g.wait_ge(S.hdl("q_z"), qz)
            g.wait_ge(S.hdl("dt_v"), 2)
            g.wait_ge(S.hdl("v_tern"), vt)
            g.wait_ge(S.hdl("t_tern"), tt)
            g.wait_ge(S.hdl("s_tern"), st_)
            g.wait_ge(S.hdl("q_tt"), qtt)
            g.memset(Hbuf[:], 0.0)
            g.memset(dagg[:], 1.0)
            g.drain().then_inc(*S.inc("b1_v"))
        G(g_init)
        b1v0 = S.val("b1_v")

        def t_gate(t, b1v0=b1v0):
            t.wait_ge(S.hdl("b1_v"), b1v0)
        TEE(t_gate)

        def v_gate(v, b1v0=b1v0):
            v.wait_ge(S.hdl("b1_v"), b1v0)
        V(v_gate)

        def s_gate(s, b1v0=b1v0):
            s.wait_ge(S.hdl("b1_v"), b1v0)
        SC(s_gate)

        def y_gate(y, b1v0=b1v0):
            y.wait_ge(S.hdl("b1_v"), b1v0)
        SY(y_gate)

        prev_sev = 0
        for c in range(NCH):
            sl = slice(c * CHUNK, (c + 1) * CHUNK)
            sc_b1 = []
            inc_d1 = S.inc("b1_s")

            def s_dec(s, c=c, sl=sl, inc=inc_d1, qdr0=S.val("q_dr")):
                if c == 0:
                    s.wait_ge(S.hdl("dt_v"), 2)
                    s.wait_ge(S.hdl("b1_v"), 1)
                else:
                    s.wait_ge(S.hdl("q_dr"), qdr0)
                s.activation(deccol[:], cum[:, sl.stop - 1:sl.stop], AF.Exp)
                s.drain()
                s.activation(dcyf[:], cum[:, sl], AF.Exp, scale=-1.0,
                             bias=cum[:, sl.stop - 1:sl.stop]).then_inc(*inc)
            sc_b1.append(s_dec)
            sdec = S.val("b1_s")

            def v_dagg(v, sdec=sdec):
                v.wait_ge(S.hdl("b1_s"), sdec)
                v.tensor_mul(dagg[:], dagg[:], deccol[:])
                v.drain().then_inc(*S.inc("b1_v"))
            V(v_dagg)

            def y_drep(y, sdec=sdec):
                y.wait_ge(S.hdl("b1_s"), sdec)
                DMA(y, "q_dr", drow[:], deccol[:])
                y.wait_ge(S.hdl("q_dr"), S.val("q_dr"))
                DMA(y, "q_dr", drep[:], drow.unsqueeze(1).broadcast_to([1, NH, NH]))
            SY(y_drep)
            qdr = S.val("q_dr")

            _, wf1 = PF.acquire()

            def t_dt(t, c=c, sl=sl, sdec=sdec, wf1=wf1):
                if c == 0:
                    t.wait_ge(S.hdl("cv"), 48)
                    t.wait_ge(S.hdl("dt_v"), 2)
                t.wait_ge(S.hdl("b1_s"), sdec)
                t.wait_ge(S.hdl("pfe"), wf1)
                t.transpose(ptr_f[0:128, 0:64], dt_sb[0:NH, sl],
                            eyef[0:NH, 0:NH]).then_inc(*S.inc("b1_t"))
            TEE(t_dt)
            t1v = S.val("b1_t")

            def v_dt(v, t1v=t1v):
                v.wait_ge(S.hdl("b1_t"), t1v)
                v.tensor_copy(ctm[:, 2, :], ptr_f[0:128, 0:64]
                              ).then_inc(*S.inc("b1_v"))
                v.drain().then_inc(*S.inc("pfe"))
            V(v_dt)
            vdt_ = S.val("b1_v")
            PF.release(0)
            v2v = S.val("b1_v")
            _, wf2 = PF.acquire()

            def t_dcy(t, v2v=v2v, wf2=wf2):
                t.wait_ge(S.hdl("b1_v"), v2v)
                t.wait_ge(S.hdl("pfe"), wf2)
                t.transpose(ptr_f[0:128, 0:64], dcyf[0:NH, :],
                            eyef[0:NH, 0:NH]).then_inc(*S.inc("b1_t"))
            TEE(t_dcy)
            t2v = S.val("b1_t")

            def v_dcy(v, t2v=t2v, vdt_=vdt_):
                v.wait_ge(S.hdl("b1_t"), t2v)
                v.tensor_copy(ctm[:, 3, :], ptr_f[0:128, 0:64]
                              ).then_inc(*S.inc("b1_v"))
                v.drain().then_inc(*S.inc("pfe"))
            V(v_dcy)
            PF.release(0)
            v3v = S.val("b1_v")

            for xt in range(8):
                bt = 8 + xt
                _, wx = PB.acquire()

                def t_x(t, c=c, sl=sl, xt=xt, bt=bt, v3v=v3v, wx=wx,
                        psev=prev_sev):
                    t.wait_ge(S.hdl("b1_v"), v3v)
                    t.wait_ge(S.hdl("pbe"), wx)
                    if psev:
                        t.wait_ge(S.hdl("b1_s"), psev)
                    t.transpose(ptr_b[0:128, 0:128], xbc[:, xt, sl], eye[:])
                    t.transpose(ptr_b[0:128, 128:256], xbc[:, bt, sl], eye[:]
                                ).then_inc(*S.inc("b1_t"))
                TEE(t_x)
                t3v = S.val("b1_t")

                inc_e1 = S.inc("b1_s")
                inc_e2 = S.inc("pbe")

                def s_ev(s, xt=xt, t3v=t3v, inc_a=inc_e1, inc_b=inc_e2):
                    s.wait_ge(S.hdl("b1_t"), t3v)
                    for hh in range(8):
                        h = 8 * xt + hh
                        xcol = (hh // 4) * 64
                        s.activation(xdt_tm[:, hh, :],
                                     ptr_b[0:128, xcol:xcol + 64], AF.Copy,
                                     scale=ctm[:, 2, h:h + 1])
                        s.activation(bdec_tm[:, hh, :],
                                     ptr_b[0:128, 128 + xcol:128 + xcol + 64],
                                     AF.Copy, scale=ctm[:, 3, h:h + 1])
                    s.drain().then_inc(*inc_a)
                    s.drain().then_inc(*inc_b)
                sc_b1.append(s_ev)
                prev_sev = S.val("b1_s")
                s2v = S.val("b1_s")

                def t_smm(t, s2v=s2v):
                    t.wait_ge(S.hdl("b1_s"), s2v)
                    t.wait_ge(S.hdl("b1_v"), S.cnt["b1_v"])
                    for hh in range(8):
                        ins = t.matmul(pss[0:64, hh * 64:hh * 64 + 64],
                                       bdec_tm[:, hh, :], xdt_tm[:, hh, :],
                                       start=True, stop=True)
                    ins.then_inc(*S.inc("b1_t"))
                TEE(t_smm)
                t4v = S.val("b1_t")

                def v_srec(v, c=c, xt=xt, t4v=t4v, qdr=qdr,
                           qsd0=S.val("q_sd")):
                    v.wait_ge(S.hdl("b1_t"), t4v)
                    if xt == 0:
                        v.wait_ge(S.hdl("q_sd"), qsd0)
                        v.wait_ge(S.hdl("q_dr"), qdr)
                    v.tensor_copy(Schk[:, xt * 512:(xt + 1) * 512], pss[0:64, :])
                    for hh in range(8):
                        h = 8 * xt + hh
                        ins = v.scalar_tensor_tensor(
                            Hbuf[:, h * 64:h * 64 + 64],
                            Hbuf[:, h * 64:h * 64 + 64],
                            drep[:, h:h + 1],
                            pss[0:64, hh * 64:hh * 64 + 64],
                            ALU.mult, ALU.add)
                    ins.then_inc(*S.inc("b1_v"))
                V(v_srec)

            def s_b1chunk(s, ops=list(sc_b1)):
                for f in ops:
                    f(s)
            SC(s_b1chunk)
            vS = S.val("b1_v")

            def y_schk(y, c=c, vS=vS):
                y.wait_ge(S.hdl("b1_v"), vS)
                DMA(y, "q_sd", s_dram[c], Schk[:])
            SY(y_schk)

        vH = S.val("b1_v")
        qsd = S.val("q_sd")

        def y_cc(y, vH=vH, qsd=qsd):
            y.wait_ge(S.hdl("b1_v"), vH)
            y.wait_ge(S.hdl("q_sd"), qsd)
            DMA(y, "q_cc", cc_in[0:1, 0:CCH], Hbuf[:])
            DMA(y, "q_cc", cc_in[0:1, CCH:], dagg[:])
        SY(y_cc)
        qcc = S.val("q_cc")

        if "cc" in debug_taps:
            tpc = tap("cc", [1, CCH + NH])

            def g_tapc(g, qcc=qcc):
                g.wait_ge(S.hdl("q_cc"), qcc)
                DMA(g, "q_tap", tpc, cc_in)
            G(g_tapc)

        def g_cc(g, qcc=qcc):
            g.wait_ge(S.hdl("q_cc"), qcc)
            g.collective_compute(
                "AllGather", ALU.bypass,
                ins=[cc_in],
                outs=[cc_out],
                replica_groups=[[0, 1, 2, 3], [4, 5, 6, 7]],
            ).then_inc(*S.inc("cc_done"))
        G(g_cc)

    def emit_hin(Hbuf, HgF, row64, erep, crep):
        ccd = S.val("cc_done")

        def v_zero(v, ccd=ccd):
            v.wait_ge(S.hdl("cc_done"), ccd)
            v.memset(Hbuf[:], 0.0)
            v.drain().then_inc(*S.inc("hin_v"))
        V(v_zero)

        for m in range(4):
            hz = S.val("hin_v")

            def y_ld(y, m=m, ccd=ccd, hz=hz):
                y.wait_ge(S.hdl("cc_done"), ccd)
                y.wait_ge(S.hdl("hin_v"), hz)
                DMA(y, "q_hg", HgF[0:NH, 0:CCH // NH], cc_out[m:m + 1, 0:CCH])
                DMA(y, "q_hg", row64[:], cc_out[m:m + 1, CCH:])
                DMA(y, "q_hg", crep[:], selc[0:1, m:m + 1].unsqueeze(1)
                            .broadcast_to([1, NH, 1]))
            SY(y_ld)
            qhg = S.val("q_hg")

            def v_e(v, m=m, qhg=qhg):
                v.wait_ge(S.hdl("q_hg"), qhg)
                v.tensor_scalar(row64[:], row64[:], selc[0:1, m:m + 1],
                                omc[0:1, m:m + 1], ALU.mult, ALU.add)
                v.drain().then_inc(*S.inc("hin_v"))
            V(v_e)
            hv = S.val("hin_v")

            def y_erep(y, hv=hv):
                y.wait_ge(S.hdl("hin_v"), hv)
                DMA(y, "q_hg", erep[:], row64.unsqueeze(1).broadcast_to([1, NH, NH]))
            SY(y_erep)
            qhg2 = S.val("q_hg")

            def v_comb(v, m=m, qhg2=qhg2):
                v.wait_ge(S.hdl("q_hg"), qhg2)
                v.tensor_scalar(HgF[0:NH, 0:CCH // NH], HgF[0:NH, 0:CCH // NH],
                                crep[:, 0:1], None, ALU.mult)
                v.drain()
                for h in range(NH):
                    ins = v.scalar_tensor_tensor(
                        Hbuf[:, h * 64:h * 64 + 64],
                        Hbuf[:, h * 64:h * 64 + 64],
                        erep[:, h:h + 1],
                        HgF[0:NH, h * 64:h * 64 + 64], ALU.mult, ALU.add)
                ins.then_inc(*S.inc("hin_v"))
            V(v_comb)

    def emit_B2(Hbuf, Hbf, Schk, cumb, zraw, zs_tm, zsl, yzc,
                nrmc, sqp, nrf4, xdt_tm, xtm, bsc, csc, Xt, Mt, gmb,
                tcor, yv, yv2, deccol, drow, drep):
        hin_done = S.val("hin_v")
        zsil_done = {}
        vx31_done = {}

        for c in range(NCH):
            sl = slice(c * CHUNK, (c + 1) * CHUNK)
            base3 = PRE + c * CHUNK
            sc_defer = []

            prev_hb = S.val("b2_hb")

            inc_dec = S.inc("b2_s")

            def s_dec(s, c=c, sl=sl, inc=inc_dec, qdr=S.cnt["q_dr"]):
                if c == 0:
                    s.wait_ge(S.hdl("hin_v"), hin_done)
                s.wait_ge(S.hdl("q_dr"), qdr)
                s.activation(deccol[:], cum[:, sl.stop - 1:sl.stop], AF.Exp
                             ).then_inc(*inc)
            sc_defer.append(s_dec)
            sdec = S.val("b2_s")

            def y_pre(y, c=c, sdec=sdec, prev_hb=prev_hb):
                y.wait_ge(S.hdl("b2_hb"), prev_hb)
                DMA(y, "q_sd", Schk[:], s_dram[c])
                y.wait_ge(S.hdl("b2_s"), sdec)
                DMA(y, "q_dr", drow[:], deccol[:])
                y.wait_ge(S.hdl("q_dr"), S.val("q_dr"))
                DMA(y, "q_dr", drep[:], drow.unsqueeze(1).broadcast_to([1, NH, NH]))
                y.wait_ge(S.hdl("b2_s"), zsil_done.get(c - 1, 0))
                Z3 = z_dram.rearrange("(zt p) t -> p zt t", p=128)
                for zz in range(4):
                    DMA(y, "q_zr", zraw[:, zz * 8:(zz + 1) * 8, :], Z3[:, zz * 8:(zz + 1) * 8,
                                       base3:base3 + 128])
                    y.nop()
            SY(y_pre)
            q_sd_c = S.val("q_sd")
            q_zr_c = S.val("q_zr")
            q_dr_c = S.val("q_dr")

            _, wf3 = PF.acquire()

            def t_cum(t, c=c, sl=sl, wf3=wf3):
                t.wait_ge(S.hdl("pfe"), wf3)
                t.transpose(ptr_f[0:128, 0:64], cum[0:NH, sl],
                            eyef[0:NH, 0:NH]).then_inc(*S.inc("b2_t"))
            TEE(t_cum)
            tcv = S.val("b2_t")

            def v_ncum(v, tcv=tcv):
                v.wait_ge(S.hdl("b2_t"), tcv)
                v.tensor_scalar_mul(ctm[:, 0, :], ptr_f[0:128, 0:64], -1.0)
                v.drain().then_inc(*S.inc("b2_v"))
                v.drain().then_inc(*S.inc("pfe"))
            V(v_ncum)

            inc_xa = S.inc("b2_s")
            inc_xb = S.inc("pfe")

            def s_exp(s, tcv=tcv, inc_a=inc_xa, inc_b=inc_xb):
                s.wait_ge(S.hdl("b2_t"), tcv)
                s.activation(ctm[:, 1, :], ptr_f[0:128, 0:64], AF.Exp
                             ).then_inc(*inc_a)
                s.drain().then_inc(*inc_b)
            sc_defer.append(s_exp)
            PF.release(0)
            vnc = S.val("b2_v")
            sxc = S.val("b2_s")
            _, wf4 = PF.acquire()

            def t_dt2(t, sl=sl, vnc=vnc, sxc=sxc, wf4=wf4):
                t.wait_ge(S.hdl("b2_v"), vnc)
                t.wait_ge(S.hdl("b2_s"), sxc)
                t.wait_ge(S.hdl("pfe"), wf4)
                t.transpose(ptr_f[0:128, 0:64], dt_sb[0:NH, sl],
                            eyef[0:NH, 0:NH]).then_inc(*S.inc("b2_t"))
            TEE(t_dt2)
            tdt = S.val("b2_t")

            def v_dt2(v, tdt=tdt):
                v.wait_ge(S.hdl("b2_t"), tdt)
                v.tensor_copy(ctm[:, 2, :], ptr_f[0:128, 0:64])
                v.drain()
                v.tensor_copy(Hbf[:], Hbuf[:])
                v.drain().then_inc(*S.inc("b2_v"))
                v.drain().then_inc(*S.inc("pfe"))
            V(v_dt2)
            PF.release(0)
            vpre = S.val("b2_v")

            def y_cb(y, c=c, sl=sl, vx31=vx31_done.get(c - 1, 0)):
                if vx31:
                    y.wait_ge(S.hdl("b2_v"), vx31)
                DMA(y, "q_cb", cumrow_dram, cum[:, sl])
                y.wait_ge(S.hdl("q_cb"), S.val("q_cb"))
                DMA(y, "q_cb", cumb[:], cumrow_dram[0:1, 0:4096].unsqueeze(1)
                            .broadcast_to([1, 128, 4096]))
            SY(y_cb)
            q_cb_h0 = S.val("q_cb")

            for zq in range(8):
                inc_z = S.inc("b2_s")

                def s_z(s, zq=zq, q_zr_c=q_zr_c, inc=inc_z,
                        zgate=S.cnt["b2_zs"], tgate=S.cnt["b2_t"]):
                    if zq == 0:
                        s.wait_ge(S.hdl("q_zr"), q_zr_c)
                        s.wait_ge(S.hdl("b2_zs"), zgate)
                    else:
                        s.wait_ge(S.hdl("b2_t"), tgate)
                    s.activation(zsl.rearrange("p a b -> p (a b)"),
                                 zraw.rearrange("p a b -> p (a b)")
                                 [:, zq * 512:(zq + 1) * 512], AF.Silu
                                 ).then_inc(*inc)
                sc_defer.append(s_z)
                szt = S.val("b2_s")
                _, wlz = PB.acquire()

                def t_z(t, zq=zq, szt=szt, wlz=wlz):
                    t.wait_ge(S.hdl("b2_s"), szt)
                    t.wait_ge(S.hdl("pbe"), wlz)
                    for d in range(4):
                        ins = t.transpose(
                            ptr_b[0:128, d * 128:d * 128 + 128],
                            zsl[:, d, :], eye[:])
                    ins.then_inc(*S.inc("b2_t"))
                TEE(t_z)
                tzt = S.val("b2_t")

                inc_zc = S.inc("b2_zs")
                inc_zp = S.inc("pbe")

                def s_zc(s, zq=zq, tzt=tzt, inc=inc_zc, inc_p=inc_zp):
                    s.wait_ge(S.hdl("b2_t"), tzt)
                    s.copy(zs_tm[:, zq * 4:zq * 4 + 4, :],
                           ptr_b.ap().rearrange("p (a b) -> p a b", a=4)
                           ).then_inc(*inc)
                    s.drain().then_inc(*inc_p)
                sc_defer.append(s_zc)
                PB.release(0)
            zsil_done[c] = S.val("b2_s")

            exp_done = {}
            gm_done = {}
            ty_done = {}
            vX_done = {}
            ye_done = {}

            def pair_stage1(P):
                h0 = 2 * P
                m = P // 2
                brow = (P // 2 % 2) * 64
                bt = 8 + m // 2
                ct = 16 + P
                par = P % 2
                q_cb_cur = S.val("q_cb")

                def v_X(v, c=c, sl=sl, P=P, h0=h0, q_cb_cur=q_cb_cur,
                        vpre=vpre, brow=brow, bt=bt, ct=ct, par=par,
                        xfree=exp_done.get(P - 2, 0),
                        cfree=ty_done.get(P - 2, 0)):
                    v.wait_ge(S.hdl("q_cb"), q_cb_cur)
                    v.wait_ge(S.hdl("b2_v"), vpre)
                    if xfree:
                        v.wait_ge(S.hdl("b2_s"), xfree)
                    for hi in range(2):
                        h = h0 + hi
                        hr = h % 32
                        ins = v.scalar_tensor_tensor(
                            Xt[:, par, hi, :],
                            cumb[:, hr * 128:hr * 128 + 128],
                            ctm[:, 0, h:h + 1], msk[:], ALU.add, ALU.add)
                    ins.then_inc(*S.inc("b2_v"))
                V(v_X)
                vX = S.val("b2_v")
                vX_done[P] = vX

                inc_M = S.inc("b2_s")

                def s_M(s, vX=vX, inc=inc_M, par=par, c=c, sl=sl,
                        brow=brow, bt=bt, ct=ct,
                        mfree=gm_done.get(P - 2, 0),
                        cfree=ty_done.get(P - 2, 0)):
                    s.wait_ge(S.hdl("b2_v"), vX)
                    if mfree:
                        s.wait_ge(S.hdl("b2_gm"), mfree)
                    if cfree:
                        s.wait_ge(S.hdl("b2_t"), cfree)
                    if brow == 64:
                        s.copy(bsc[:, par, :], xbc[64:128, bt, sl])
                    s.copy(csc[:, par, :], xbc[64:128, ct, sl])
                    s.activation(Mt[:, par, 0, :], Xt[:, par, 0, :], AF.Exp)
                    s.activation(Mt[:, par, 1, :], Xt[:, par, 1, :], AF.Exp
                                 ).then_inc(*inc)
                sc_defer.append(s_M)
                exp_done[P] = S.val("b2_s")

                def t_g(t, c=c, sl=sl, P=P, vX=vX, par=par,
                        brow=brow, bt=bt, ct=ct, ge_=(P % 2 == 0),
                        xr_s=exp_done[P],
                        gfree=gm_done.get(P - 2, 0)):
                    t.wait_ge(S.hdl("b2_s"), xr_s)
                    if gfree:
                        t.wait_ge(S.hdl("b2_gm"), gfree)
                    gb = pgg if ge_ else pss
                    Bop = bsc[:, par, :] if brow == 64 else xbc[0:64, bt, sl]
                    for hi in range(2):
                        Cop = csc[:, par, :] if hi == 1 else xbc[0:64, ct, sl]
                        ins = t.matmul(
                            gb[0:128, hi * 128:hi * 128 + 128],
                            Bop, Cop, start=True, stop=True)
                    ins.then_inc(*S.inc("b2_t"))
                TEE(t_g)
                return S.val("b2_t")

            def pair_stage2a(P, tg):
                h0 = 2 * P
                ct = 16 + P
                par = P % 2
                sM = exp_done[P]

                def v_gm(v, tg=tg, sM=sM, ge_=(P % 2 == 0), par=par,
                         gfree=ty_done.get(P - 2, 0)):
                    v.wait_ge(S.hdl("b2_t"), tg)
                    v.wait_ge(S.hdl("b2_s"), sM)
                    if gfree:
                        v.wait_ge(S.hdl("b2_t"), gfree)
                    gb = pgg if ge_ else pss
                    for hi in range(2):
                        ins = v.tensor_mul(gmb[:, par, hi, :],
                                           gb[0:128, hi * 128:hi * 128 + 128],
                                           Mt[:, par, hi, :])
                    ins.then_inc(*S.inc("b2_gm"))
                V(v_gm)
                gm_done[P] = S.val("b2_gm")
                vGM = S.val("b2_gm")

                ybk = pb[P % 2]
                cbk = pb[2 + P % 2]

                def t_y(t, P=P, h0=h0, vGM=vGM, ybk=ybk, cbk=cbk, c=c, sl=sl,
                        ct=ct, yfree=ye_done.get(P - 2, 0)):
                    t.wait_ge(S.hdl("b2_gm"), vGM)
                    if yfree:
                        t.wait_ge(S.hdl("b2_ye"), yfree)
                    par = P % 2
                    for hi in range(2):
                        Cop = csc[:, par, :] if hi == 1 else xbc[0:64, ct, sl]
                        t.matmul(ybk[0:128, hi * 64:hi * 64 + 64],
                                 gmb[:, par, hi, :],
                                 xdt_tm[:, (h0 + hi) % 8, :],
                                 start=True, stop=True)
                        ins = t.matmul(cbk[0:128, hi * 64:hi * 64 + 64],
                                       Cop,
                                       Hbf[:, (h0 + hi) * 64:(h0 + hi) * 64 + 64],
                                       start=True, stop=True)
                    ins.then_inc(*S.inc("b2_t"))
                TEE(t_y)
                ty_done[P] = S.val("b2_t")

            def pair_stage2b(P):
                h0 = 2 * P
                ybk = pb[P % 2]
                cbk = pb[2 + P % 2]
                ty = ty_done[P]

                def v_y(v, ty=ty, h0=h0, ybk=ybk, cbk=cbk, xr_v=S.val("b2_xt"),
                        c=c, P=P, qgbc=S.val("q_gbc")):
                    if c == 0 and P == 0:
                        v.wait_ge(S.hdl("q_gbc"), qgbc)
                    v.wait_ge(S.hdl("b2_t"), ty)
                    v.wait_ge(S.hdl("b2_xt"), xr_v)
                    for hi in range(2):
                        h = h0 + hi
                        xcol = ((h % 8) // 4) * 64
                        v.scalar_tensor_tensor(yv[:, hi * 64:hi * 64 + 64],
                                               xtm[:, xcol:xcol + 64],
                                               D_rep[:, h:h + 1],
                                               ybk[0:128, hi * 64:hi * 64 + 64],
                                               ALU.mult, ALU.add)
                    v.drain()
                    for hi in range(2):
                        h = h0 + hi
                        v.scalar_tensor_tensor(yv2[:, hi * 64:hi * 64 + 64],
                                               cbk[0:128, hi * 64:hi * 64 + 64],
                                               ctm[:, 1, h:h + 1],
                                               yv[:, hi * 64:hi * 64 + 64],
                                               ALU.mult, ALU.add)
                    v.drain()
                    v.wait_ge(S.hdl("b2_zs"), 8 * c + P // 4 + 1)
                    v.tensor_mul(yzc[:, h0 * 64:h0 * 64 + 128], yv2[:],
                                 zs_tm.rearrange("p a b -> p (a b)")
                                 [:, h0 * 64:h0 * 64 + 128]
                                 ).then_inc(*S.inc("b2_ye"))
                V(v_y)
                ye_done[P] = S.val("b2_ye")

            for xt in range(8):
                P0 = 4 * xt
                if P0 == 16:
                    def y_cb2(y, c=c, sl=sl, vx15=vX_done[15]):
                        y.wait_ge(S.hdl("b2_v"), vx15)
                        DMA(y, "q_cb", cumb[:], cumrow_dram[0:1, 4096:8192].unsqueeze(1)
                                    .broadcast_to([1, 128, 4096]))
                    SY(y_cb2)

                slx, wlx = PB.acquire()

                def t_x2(t, c=c, sl=sl, xt=xt, slx=slx, wlx=wlx,
                         tyf=ty_done.get(4 * xt - 1, 0),
                         vgate=S.cnt["b2_v"]):
                    t.wait_ge(S.hdl("b2_v"), vgate)
                    t.wait_ge(S.hdl("pbe"), wlx)
                    t.transpose(ptr_b[0:128, slx * 128:slx * 128 + 128],
                                xbc[:, xt, sl],
                                eye[:]).then_inc(*S.inc("b2_t"))
                TEE(t_x2)
                txt = S.val("b2_t")

                def v_x2(v, xt=xt, txt=txt, slx=slx,
                         tyf=ty_done.get(4 * xt - 1, 0)):
                    v.wait_ge(S.hdl("b2_t"), txt)
                    if tyf:
                        v.wait_ge(S.hdl("b2_t"), tyf)
                    v.tensor_copy(xtm[:], ptr_b[0:128, slx * 128:slx * 128 + 128])
                    for hh in range(8):
                        h = 8 * xt + hh
                        xcol = (hh // 4) * 64
                        v.tensor_scalar(xdt_tm[:, hh, :],
                                        ptr_b[0:128, slx * 128 + xcol:
                                              slx * 128 + xcol + 64],
                                        ctm[:, 2, h:h + 1], None,
                                        ALU.mult)
                    v.drain().then_inc(*S.inc("b2_xt"))
                    v.drain().then_inc(*S.inc("pbe"))
                V(v_x2)
                PB.release(slx)

                tg0 = pair_stage1(P0)
                tg1 = pair_stage1(P0 + 1)
                pair_stage2a(P0, tg0)
                tg2 = pair_stage1(P0 + 2)
                pair_stage2a(P0 + 1, tg1)
                pair_stage2b(P0)
                tg3 = pair_stage1(P0 + 3)
                pair_stage2a(P0 + 2, tg2)
                pair_stage2b(P0 + 1)
                pair_stage2a(P0 + 3, tg3)
                pair_stage2b(P0 + 2)
                pair_stage2b(P0 + 3)
            vx31_done[c] = vX_done[31]

            vyz = S.val("b2_ye")

            for sq in range(8):
                inc_q = S.inc("b2_s")

                def s_sq(s, sq=sq, vyz=vyz, inc=inc_q, vg=S.cnt["b2_v"]):
                    s.wait_ge(S.hdl("b2_ye"), vyz)
                    if sq > 1:
                        s.wait_ge(S.hdl("b2_v"), vg)
                    s.activation(sqp[:, sq % 2, :],
                                 yzc[:, sq * 512:(sq + 1) * 512],
                                 AF.Square).then_inc(*inc)
                sc_defer.append(s_sq)
                ssq = S.val("b2_s")

                def v_sq(v, sq=sq, ssq=ssq):
                    v.wait_ge(S.hdl("b2_s"), ssq)
                    v.tensor_reduce(ctm[:, 4, sq * 8:(sq + 1) * 8],
                                    sqp[:, sq % 2, :].rearrange(
                                        "p (h d) -> p h d", h=8),
                                    AX.X, ALU.add)
                    v.drain().then_inc(*S.inc("b2_v"))
                V(v_sq)
            vred = S.val("b2_v")

            def v_rs(v, vred=vred):
                v.wait_ge(S.hdl("b2_v"), vred)
                v.tensor_scalar(ctm[:, 4, :], ctm[:, 4, :], 1.0 / 64, EPS,
                                ALU.mult, ALU.add)
                v.drain()
                v.reciprocal(ctm[:, 5, :], ctm[:, 4, :])
                v.drain().then_inc(*S.inc("b2_v"))
            V(v_rs)
            vrr = S.val("b2_v")

            inc_rs = S.inc("b2_s")

            def s_rs(s, vrr=vrr, inc=inc_rs):
                s.wait_ge(S.hdl("b2_v"), vrr)
                s.activation(ctm[:, 5, :], ctm[:, 5, :], AF.Sqrt
                             ).then_inc(*inc)
            sc_defer.append(s_rs)
            srr = S.val("b2_s")

            def v_nm(v, srr=srr):
                v.wait_ge(S.hdl("b2_s"), srr)
                for h in range(NH):
                    ins = v.scalar_tensor_tensor(
                        nrmc[:, h * 64:h * 64 + 64],
                        yzc[:, h * 64:h * 64 + 64],
                        ctm[:, 5, h:h + 1],
                        nw_bc[:, h * 64:h * 64 + 64], ALU.mult, ALU.mult)
                ins.then_inc(*S.inc("b2_nm"))
            V(v_nm)
            vnm = S.val("b2_nm")

            for fg in range(8):
                _, wnf = PB.acquire()

                def t_nf(t, fg=fg, vnm=vnm, wnf=wnf):
                    t.wait_ge(S.hdl("b2_nm"), vnm)
                    t.wait_ge(S.hdl("pbe"), wnf)
                    for d in range(4):
                        ft = fg * 4 + d
                        ins = t.transpose(
                            ptr_b[0:128, d * 128:d * 128 + 128],
                            nrmc[:, ft * 128:(ft + 1) * 128], eye[:])
                    ins.then_inc(*S.inc("b2_t"))
                TEE(t_nf)
                tnf = S.val("b2_t")

                inc_nc = S.inc("b2_nm")
                inc_np = S.inc("pbe")

                def s_nf(s, fg=fg, tnf=tnf, inc=inc_nc, inc_p=inc_np,
                         qg=max(0, S.cnt["q_nr"] - 16)):
                    s.wait_ge(S.hdl("b2_t"), tnf)
                    s.wait_ge(S.hdl("q_nr"), qg)
                    s.copy(nrf4[:, fg % 2, :, :],
                           ptr_b.ap().rearrange("p (a b) -> p a b", a=4)
                           ).then_inc(*inc)
                    s.drain().then_inc(*inc_p)
                sc_defer.append(s_nf)
                PB.release(0)
                vnf = S.val("b2_nm")

                def y_nf(y, c=c, fg=fg, vnf=vnf):
                    y.wait_ge(S.hdl("b2_nm"), vnf)
                    DMA(y, "q_nr",
                        nrm_dram.rearrange("(ft p) t -> p ft t", p=128)
                        [:, fg * 4:fg * 4 + 4, c * CHUNK:(c + 1) * CHUNK],
                        nrf4[:, fg % 2, :, :])
                SY(y_nf)

            def s_chunk(s, ops=list(sc_defer)):
                for f in ops:
                    f(s)
            SC(s_chunk)

            def v_h(v, q_sd_c=q_sd_c, q_dr_c=q_dr_c):
                v.wait_ge(S.hdl("q_sd"), q_sd_c)
                v.wait_ge(S.hdl("q_dr"), q_dr_c)
                for h in range(NH):
                    ins = v.scalar_tensor_tensor(
                        Hbuf[:, h * 64:h * 64 + 64],
                        Hbuf[:, h * 64:h * 64 + 64],
                        drep[:, h:h + 1],
                        Schk[:, h * 64:h * 64 + 64], ALU.mult, ALU.add)
                ins.then_inc(*S.inc("b2_hb"))
            V(v_h)

        if "nrm" in debug_taps:
            tpn = tap("nrm", [DIN, T], BF16)
            qnr = S.val("q_nr")

            def g_tapn(g, qnr=qnr):
                g.wait_ge(S.hdl("q_nr"), qnr)
                DMA(g, "q_tap", tpn, nrm_dram)
            G(g_tapn)

    def emit_C(r2, nrh, wob, q1y, q1i, q1r, xq_tm, xqf, cstop=None):
        qnr_all = S.val("q_nr")
        qwob_all = S.val("q_wob")
        mm_done = {}
        kb_done = {}
        ev_done = {}
        def emit_c_mm(hf):
            def y_nrh(y, hf=hf, qnr_all=qnr_all, qwob_all=qwob_all,
                      prevmm=mm_done.get((0, 3), 0)):
                if hf == 0:
                    y.wait_ge(S.hdl("fin"), 1)
                    y.wait_ge(S.hdl("q_nr"), qnr_all)
                    y.wait_ge(S.hdl("q_wob"), qwob_all)
                else:
                    y.wait_ge(S.hdl("c_t"), prevmm)
                N3 = nrm_dram.rearrange("(ft p) t -> p ft t", p=128)
                for zz in range(4):
                    DMA(y, "q_nr", nrh[:, zz * 8:(zz + 1) * 8, :], N3[:, zz * 8:(zz + 1) * 8,
                                       hf * 512:hf * 512 + 512])
                    y.nop()
                for ts4 in range(4):
                    DMA(y, "q_ur", r2[:, hf * 4 + ts4, :],
                        u_in[PRE + hf * 512 + ts4 * 128:
                             PRE + hf * 512 + ts4 * 128 + 128, :])
            SY(y_nrh)
            qnr = S.val("q_nr")
            qur = S.val("q_ur")

            for db in range(4):
                for kb in range(4):
                    if kb >= 2:
                        tfree = kb_done.get((hf, db, kb - 2), 0)
                    else:
                        tfree = kb_done.get((hf, db - 1, kb + 2),
                                            kb_done.get((hf - 1, 3, kb + 2),
                                                        0))

                    qwoq = "q_wo" if (db * 4 + kb) % 2 == 0 else "q_wob2"

                    def y_wo(y, hf=hf, db=db, kb=kb, tfree=tfree, qwoq=qwoq):
                        W3b = wob_dram.rearrange("(kt p) d -> p kt d", p=128)
                        if tfree:
                            y.wait_ge(S.hdl("c_t"), tfree)
                        DMA(y, qwoq, wob[:, kb % 2, :, :],
                            W3b[:, kb * 8:(kb + 1) * 8,
                                db * 512:(db + 1) * 512])
                        y.nop()
                    SY(y_wo)
                    qwo = (qwoq, S.val(qwoq))

                    for tt4 in range(4):
                        def t_mm(t, hf=hf, db=db, kb=kb, tt4=tt4, qnr=qnr,
                                 qwo=qwo,
                                 evfree=ev_done.get((hf, db - 1, tt4), 0) if
                                 (hf, db - 1, tt4) in ev_done else
                                 ev_done.get((hf - 1, 3, tt4), 0)):
                            if kb == 0 and tt4 == 0:
                                t.wait_ge(S.hdl("q_nr"), qnr)
                            t.wait_ge(S.hdl(qwo[0]), qwo[1])
                            if kb == 0 and evfree:
                                t.wait_ge(S.hdl("c_v"), evfree)
                            bank = pb[tt4]
                            for j in range(8):
                                ins = t.matmul(
                                    bank[0:128, :],
                                    nrh[:, kb * 8 + j, tt4 * 128:tt4 * 128 + 128],
                                    wob[:, kb % 2, j, :],
                                    start=(kb == 0 and j == 0),
                                    stop=(kb == 3 and j == 7))
                            if tt4 == 3:
                                ins.then_inc(*S.inc("c_t"))
                        TEE(t_mm)
                    kb_done[(hf, db, kb)] = S.val("c_t")
                mm_done[(hf, db)] = S.val("c_t")
                tds = S.val("c_t")

                for tt4 in range(4):
                    def v_r1(v, hf=hf, db=db, tt4=tt4, tds=tds, qur=qur):
                        v.wait_ge(S.hdl("c_t"), tds)
                        v.wait_ge(S.hdl("q_ur"), qur)
                        v.tensor_add(
                            r2[:, hf * 4 + tt4, db * 512:(db + 1) * 512],
                            r2[:, hf * 4 + tt4, db * 512:(db + 1) * 512],
                            pb[tt4][0:128, :]
                            ).then_inc(*S.inc("c_v"))
                    V(v_r1)
                    ev_done[(hf, db, tt4)] = S.val("c_v")

            if cstop == "mm":
                return

        def emit_c_ln(hf):
            r1v = S.val("c_v")
            sc_ln1 = []
            xqev = {}
            for ts4 in range(4):
                tt = hf * 4 + ts4

                inc_sq = S.inc("c_s")
                sc_ln1.append(("sq", tt, r1v, S.val("c_v"), inc_sq))
                csq = S.val("c_s")

                def v_ln(v, tt=tt, csq=csq):
                    v.wait_ge(S.hdl("c_s"), csq)
                    v.reduce_sum(smallq[:, 0:1], q1y[:], AX.X)
                    v.drain()
                    v.reduce_sum(smallq[:, 1:2], r2[:, tt, :], AX.X)
                    v.drain()
                    v.tensor_scalar_mul(smallq[:, 2:3], smallq[:, 1:2], 1.0 / DM)
                    v.drain()
                    v.tensor_mul(smallq[:, 3:4], smallq[:, 2:3], smallq[:, 2:3])
                    v.drain()
                    v.scalar_tensor_tensor(smallq[:, 4:5], smallq[:, 0:1],
                                           1.0 / DM, smallq[:, 3:4],
                                           ALU.mult, ALU.subtract)
                    v.drain()
                    v.tensor_scalar_add(smallq[:, 4:5], smallq[:, 4:5], EPS)
                    v.drain()
                    v.reciprocal(smallq[:, 5:6], smallq[:, 4:5])
                    v.drain().then_inc(*S.inc("c_v"))
                V(v_ln)
                cln = S.val("c_v")
                inc_rt = S.inc("c_s")
                sc_ln1.append(("rt", tt, cln, 0, inc_rt))
                cs2 = S.val("c_s")

                def v_ap(v, tt=tt, cs2=cs2):
                    v.wait_ge(S.hdl("c_s"), cs2)
                    v.tensor_scalar(r2[:, tt, :], r2[:, tt, :], smallq[:, 2:3],
                                    smallq[:, 5:6], ALU.subtract, ALU.mult)
                    v.drain()
                    v.reduce_max(smallq[:, 6:7], r2[:, tt, :], AX.X,
                                 apply_absolute_value=True)
                    v.drain()
                    v.tensor_scalar(smallq[:, 6:7], smallq[:, 6:7], 1e-5, None,
                                    ALU.max)
                    v.drain()
                    v.reciprocal(smallq[:, 7:8], smallq[:, 6:7])
                    v.drain()
                    v.tensor_scalar_mul(smallq[:, 7:8], smallq[:, 7:8], 127.0)
                    v.drain()
                    v.tensor_scalar_mul(smallq[:, 8:9], smallq[:, 6:7],
                                        -1.0 / 127.0)
                    v.tensor_copy(smallq[:, 9 + tt:10 + tt], smallq[:, 6:7])
                    v.drain()
                    v.tensor_scalar_mul(smallq[:, 9 + tt:10 + tt],
                                        smallq[:, 9 + tt:10 + tt],
                                        -1.0 / 127.0)
                    v.drain()
                    v.tensor_scalar(q1y[:], r2[:, tt, :], smallq[:, 7:8], None,
                                    ALU.mult)
                    v.drain()
                    v.tensor_copy(q1i[:], q1y[:])
                    v.drain()
                    v.tensor_copy(q1r[:], q1i[:])
                    v.drain()
                    v.tensor_sub(q1y[:], q1y[:], q1r[:])
                    v.drain()
                    v.scalar_tensor_tensor(q1r[:], q1y[:], 0.5, q1r[:],
                                           ALU.is_gt, ALU.add)
                    v.drain()
                    v.scalar_tensor_tensor(xq_tm[:], q1y[:], -0.5, q1r[:],
                                           ALU.is_lt, ALU.subtract)
                    v.drain().then_inc(*S.inc("c_v"))
                V(v_ap)
                cqv = S.val("c_v")

                if cstop == "ln":
                    continue
                for qc in range(4):
                    bank = pgg if qc % 2 == 0 else pss

                    def t_xq(t, qc=qc, cqv=cqv, bank=bank,
                             ev=xqev.get(qc % 2, 0)):
                        t.wait_ge(S.hdl("c_v"), cqv)
                        if ev:
                            t.wait_ge(S.hdl("c_v"), ev)
                        for d in range(4):
                            kq = qc * 4 + d
                            ins = t.transpose(
                                bank[0:128, d * 128:d * 128 + 128],
                                xq_tm[:, kq * 128:(kq + 1) * 128], eyef[:])
                        ins.then_inc(*S.inc("c_t"))
                    TEE(t_xq)
                    txq = S.val("c_t")

                    def v_xq(v, hf=hf, ts4=ts4, qc=qc, txq=txq, bank=bank):
                        v.wait_ge(S.hdl("c_t"), txq)
                        v.tensor_copy(
                            xqf[:, hf, qc * 4:qc * 4 + 4,
                                ts4 * 128:ts4 * 128 + 128],
                            bank.ap().rearrange("p (a b) -> p a b", a=4))
                        v.drain().then_inc(*S.inc("c_v"))
                    V(v_xq)
                    xqev[qc % 2] = S.val("c_v")

            def s_ln1all(s, ops=list(sc_ln1)):
                for (kind, tt, g1, g2, inc_) in ops:
                    if kind == "sq":
                        s.wait_ge(S.hdl("c_v"), g1)
                        s.wait_ge(S.hdl("c_v"), g2)
                        s.activation(q1y[:], r2[:, tt, :], AF.Square
                                     ).then_inc(*inc_)
                    else:
                        s.wait_ge(S.hdl("c_v"), g1)
                        s.activation(smallq[:, 5:6], smallq[:, 5:6], AF.Sqrt
                                     ).then_inc(*inc_)
            SC(s_ln1all)


        emit_c_mm(0)
        if cstop == "mm":
            return
        emit_c_ln(0)
        emit_c_mm(1)
        emit_c_ln(1)
        if "r2" in debug_taps:
            tpr = tap("r2", [128, 8, DM])
            cvv = S.val("c_v")

            def g_tapr(g, cvv=cvv):
                g.wait_ge(S.hdl("c_v"), cvv)
                DMA(g, "q_tap", tpr, r2[:])
            G(g_tapr)

    def emit_D(r2, xqf):
        tb1_off = (AR.off + 31) & ~31
        tb1 = AR.alloc([128, 2, 16, 256], BF16)
        gpiece = AR.alloc([128, 2, 256], BF16)
        q2y_off = (AR.off + 31) & ~31
        q2y = AR.alloc([128, 4096], F32)
        q2i_off = (AR.off + 31) & ~31
        q2i = AR.alloc([128, 4096], I32)
        q2r = AR.alloc([128, 4096], F32)
        gld = AR.at(tb1_off + 8192, [128, 4096], BF16)
        q2rnd = AR.at(q2i_off, [128, 4096], F32)
        q2f = AR.at(q2y_off, [128, 32, 128], BF16)
        gmx = AR.alloc([128, 8, 32], F32)
        tb2 = AR.alloc([128, 2, 2, 8, 256], BF16)
        q2l = AR.alloc([128, 3, 8, 128], BF16)
        h2b = AR.alloc([128, 512], F32)
        outb = AR.at(tb1_off, [128, DM], F32)
        gmr = AR.alloc([128, 8, 2], F32)

        mm1_bank_free = {}
        gelu_done = {}
        reduce_done = {}
        piece_qgp = {}
        q2ev_done = {}
        mm2_done = {}
        sc_gelu = []

        for hb in range(32):
            qt1_prev = S.val("q_t1")

            def g_t1(g, hb=hb,
                     cfin=(S.val("c_t"), S.val("c_v"), S.val("c_s"))):
                if hb == 0:
                    g.wait_ge(S.hdl("c_t"), cfin[0])
                    g.wait_ge(S.hdl("c_v"), cfin[1])
                    g.wait_ge(S.hdl("c_s"), cfin[2])
                    g.wait_ge(S.hdl("cc2"), 1)
                if hb >= 2:
                    g.wait_ge(S.hdl("d_t"), mm1_bank_free.get(("t", hb - 2), 0))
                DMA(g, "q_t1",
                    tb1[:, hb % 2, :, :].rearrange("p a b -> p (a b)"),
                    T1V[hb // 4][:, hb % 4, :])
                g.nop()
            G(g_t1)
            qt1 = S.val("q_t1")

            for hf in range(2):
                for tt in range(4):
                    gi = hb * 8 + hf * 4 + tt

                    def t_m1(t, hf=hf, hb=hb, tt=tt, qt1=qt1, gi=gi):
                        t.wait_ge(S.hdl("q_t1"), qt1)
                        if gi == 0:
                            t.wait_ge(S.hdl("c_v"), S.cnt["c_v"])
                        if gi - 4 in mm1_bank_free:
                            t.wait_ge(S.hdl("d_s"), mm1_bank_free[gi - 4])
                        bank = pb[tt]
                        for k in range(16):
                            ins = t.matmul(bank[0:128, 0:256],
                                           xqf[:, hf, k, tt * 128:tt * 128 + 128],
                                           tb1[:, hb % 2, k, :],
                                           start=(k == 0), stop=(k == 15))
                        ins.then_inc(*S.inc("d_t"))
                    TEE(t_m1)
                    tm1 = S.val("d_t")

                    ginc = S.inc("d_s")
                    gelu_done[gi] = S.val("d_s")
                    mm1_bank_free[gi] = S.val("d_s")
                    sc_gelu.append((tm1,
                                    piece_qgp.get(gi - 2, 0),
                                    reduce_done.get(gi - 2, 0),
                                    hb, tt, hf, ginc))
                    sg = S.val("d_s")

                    def v_gmx(v, hf=hf, hb=hb, tt=tt, sg=sg, gi=gi):
                        v.wait_ge(S.hdl("d_s"), sg)
                        v.reduce_max(gmx[:, hf * 4 + tt, hb:hb + 1],
                                     gpiece[:, gi % 2, :], AX.X,
                                     apply_absolute_value=True)
                        v.drain().then_inc(*S.inc("d_v"))
                    V(v_gmx)
                    reduce_done[gi] = S.val("d_v")

                    def y_g(y, hf=hf, hb=hb, tt=tt, sg=sg, gi=gi):
                        y.wait_ge(S.hdl("d_s"), sg)
                        DMA(y, "q_gp", g_dram[hf * 512 + tt * 128:
                                               hf * 512 + tt * 128 + 128,
                                               hb * 256:(hb + 1) * 256], gpiece[:, gi % 2, :])
                    SY(y_g)
                    piece_qgp[gi] = S.val("q_gp")
            mm1_bank_free[("t", hb)] = S.val("d_t")

        def s_gelu(s, ops=list(sc_gelu)):
            for (tm1, qgp_g, rd_g, hb, tt, hf_, ginc) in ops:
                s.wait_ge(S.hdl("d_t"), tm1)
                if qgp_g:
                    s.wait_ge(S.hdl("q_gp"), qgp_g)
                if rd_g:
                    s.wait_ge(S.hdl("d_v"), rd_g)
                gi = hb * 8 + hf_ * 4 + tt
                s.activation(gpiece[:, gi % 2, :],
                             pb[tt][0:128, 0:256], AF.Gelu_apprx_tanh,
                             scale=smallq[:, 9 + hf_ * 4 + tt:
                                          10 + hf_ * 4 + tt]
                             ).then_inc(*ginc)
        SC(s_gelu)

        sc_q2_by_hf = {0: [], 1: []}
        qgp_all = S.val("q_gp")

        def emit_q2_tt(hf, tt):
            ttg = hf * 4 + tt
            gmv = S.val("d_v")

            def v_s2(v, ttg=ttg, tt=tt, hf=hf, qgp=qgp_all, gmv=gmv):
                if hf == 0 and tt == 0:
                    v.wait_ge(S.hdl("q_gp"), qgp)
                v.wait_ge(S.hdl("d_v"), gmv)
                v.reduce_max(smallq[:, 17:18], gmx[:, ttg, :], AX.X)
                v.drain()
                v.tensor_scalar(smallq[:, 17:18], smallq[:, 17:18], 1e-5,
                                None, ALU.max)
                v.drain()
                v.reciprocal(smallq[:, 18:19], smallq[:, 17:18])
                v.drain()
                v.tensor_scalar_mul(smallq[:, 18:19], smallq[:, 18:19],
                                    127.0)
                v.drain()
                v.tensor_scalar(gmr[:, ttg, 1:2], smallq[:, 17:18],
                                -1.0 / 127.0, None, ALU.mult)
                v.drain().then_inc(*S.inc("d_v"))
            V(v_s2)

            for sub in range(2):
                qgrq = "q_gr" if (ttg * 2 + sub) % 2 == 0 else "q_grb"

                def y_gl(y, ttg=ttg, sub=sub, dvv=S.val("d_v"), qgrq=qgrq):
                    y.wait_ge(S.hdl("d_v"), dvv)
                    DMA(y, qgrq, gld[:], g_dram[ttg * 128:ttg * 128 + 128,
                                           sub * 4096:(sub + 1) * 4096])
                SY(y_gl)
                qgr = (qgrq, S.val(qgrq))

                def v_q2(v, qgr=qgr, qq2c=S.val("q_q2")):
                    v.wait_ge(S.hdl(qgr[0]), qgr[1])
                    v.wait_ge(S.hdl("q_q2"), qq2c)
                    v.tensor_scalar(q2y[:], gld[:], smallq[:, 18:19], None,
                                    ALU.mult)
                    v.drain()
                    v.tensor_copy(q2i[:], q2y[:])
                    v.drain()
                    v.tensor_copy(q2r[:], q2i[:])
                    v.drain()
                    v.tensor_sub(q2y[:], q2y[:], q2r[:])
                    v.drain()
                    v.scalar_tensor_tensor(q2r[:], q2y[:], 0.5, q2r[:],
                                           ALU.is_gt, ALU.add)
                    v.drain()
                    v.scalar_tensor_tensor(q2rnd[:], q2y[:], -0.5, q2r[:],
                                           ALU.is_lt, ALU.subtract)
                    v.drain().then_inc(*S.inc("d_v"))
                V(v_q2)
                vq2 = S.val("d_v")

                for qq in range(8):
                    bank = pgg if qq % 2 == 0 else pss

                    def t_q2(t, qq=qq, vq2=vq2, bank=bank,
                             ev=q2ev_done.get(qq % 2, 0)):
                        t.wait_ge(S.hdl("d_v"), vq2)
                        if ev:
                            t.wait_ge(S.hdl("d_s"), ev)
                        for d in range(4):
                            kq = qq * 4 + d
                            ins = t.transpose(
                                bank[0:128, d * 128:d * 128 + 128],
                                q2rnd[:, kq * 128:(kq + 1) * 128],
                                eyef[:])
                        ins.then_inc(*S.inc("d_t"))
                    TEE(t_q2)
                    tq2 = S.val("d_t")
                    einc = S.inc("d_s")
                    q2ev_done[qq % 2] = S.val("d_s")
                    sc_q2_by_hf[hf].append((tq2, qq, bank,
                                            S.val("q_q2") if qq == 0 else 0,
                                            einc))
                sq2 = S.val("d_s")

                def y_q2(y, ttg=ttg, sub=sub, sq2=sq2):
                    y.wait_ge(S.hdl("d_s"), sq2)
                    DMA(y, "q_q2", q2_dram[ttg, sub],
                        q2f.rearrange("p a b -> p (a b)"))
                SY(y_q2)

        def emit_q2_flush(hf):
            def s_q2all(s, ops=list(sc_q2_by_hf[hf])):
                for (tq2, qq, bank, qq2p, einc) in ops:
                    s.wait_ge(S.hdl("d_t"), tq2)
                    if qq2p:
                        s.wait_ge(S.hdl("q_q2"), qq2p)
                    s.copy(q2f[:, qq * 4:qq * 4 + 4, :],
                           bank.ap().rearrange("p (a b) -> p a b", a=4)
                           ).then_inc(*einc)
            SC(s_q2all)

        def emit_mm2(hf, cb=None):
            qq2 = S.val("q_q2")
            for dt_ in range(4):
                for kg in range(8):
                    gi = dt_ * 8 + kg
                    ggi = hf * 32 + gi

                    qt2q = "q_t2" if ggi % 2 == 0 else "q_t2b"

                    def g_t2(g, dt_=dt_, kg=kg, ggi=ggi, qt2q=qt2q):
                        if ggi == 0:
                            g.wait_ge(S.hdl("cc2"), 1)
                        if ggi >= 2:
                            g.wait_ge(S.hdl("d_t"), mm2_done.get(ggi - 2, 0))
                        DMA(g, qt2q, tb2[:, ggi % 2, 0, :, :],
                            T2V[dt_ * 2][:, kg * 8:(kg + 1) * 8, :])
                        g.nop()
                        DMA(g, qt2q, tb2[:, ggi % 2, 1, :, :],
                            T2V[dt_ * 2 + 1][:, kg * 8:(kg + 1) * 8, :])
                        g.nop()
                    G(g_t2)
                    qt2 = (qt2q, S.val(qt2q))

                    for tt in range(4):
                        gt_ = ggi * 4 + tt

                        qlq = "q_q2l" if gt_ % 2 == 0 else "q_q2lb"

                        def y_l(y, hf=hf, kg=kg, tt=tt, gt_=gt_, qq2=qq2,
                                dt_=dt_, qlq=qlq):
                            if dt_ == 0 and kg == 0:
                                y.wait_ge(S.hdl("q_q2"), qq2)
                            if gt_ >= 3:
                                y.wait_ge(S.hdl("d_t"), mm2_done.get(("l", gt_ - 3), 0))
                            DMA(y, qlq,
                                q2l[:, gt_ % 3, :, :].rearrange(
                                    "p a b -> p (a b)"),
                                q2_dram[hf * 4 + tt, kg // 4]
                                [:, (kg % 4) * 1024:(kg % 4) * 1024 + 1024])
                        SY(y_l)
                        ql = (qlq, S.val(qlq))

                        def t_m2(t, dt_=dt_, kg=kg, tt=tt, qt2=qt2, ql=ql,
                                 ggi=ggi, gt_=gt_, hf=hf):
                            t.wait_ge(S.hdl(qt2[0]), qt2[1])
                            t.wait_ge(S.hdl(ql[0]), ql[1])
                            if kg == 0:
                                t.wait_ge(S.hdl("d_v"),
                                          mm2_done.get(("e", hf, dt_ - 1, tt), 0))
                            bank = pb[tt]
                            for k in range(8):
                                ins = t.matmul(bank[0:128, :],
                                               q2l[:, gt_ % 3, k, :],
                                               tb2[:, ggi % 2, :, k, :],
                                               start=(kg == 0 and k == 0),
                                               stop=(kg == 7 and k == 7))
                            ins.then_inc(*S.inc("d_t"))
                        TEE(t_m2)
                        mm2_done[("l", gt_)] = S.val("d_t")
                    mm2_done[ggi] = S.val("d_t")

                for tt in range(4):
                    tm2 = S.val("d_t")

                    def v_h2(v, hf=hf, dt_=dt_, tt=tt, tm2=tm2):
                        v.wait_ge(S.hdl("d_t"), tm2)
                        v.tensor_scalar(h2b[:], pb[tt][0:128, :],
                                        gmr[:, hf * 4 + tt, 1:2], None,
                                        ALU.mult)
                        v.drain()
                        v.tensor_add(r2[:, hf * 4 + tt, dt_ * 512:(dt_ + 1) * 512],
                                     r2[:, hf * 4 + tt, dt_ * 512:(dt_ + 1) * 512],
                                     h2b[:]).then_inc(*S.inc("d_v"))
                    V(v_h2)
                    mm2_done[("e", hf, dt_, tt)] = S.val("d_v")
                if cb is not None:
                    cb(dt_)

        def emit_ln(hf):
            rdone = S.val("d_v")
            qq2_gate = S.val("q_q2")
            sc_ln2 = []
            for tt in range(4):
                ttg = hf * 4 + tt

                inc_sq = S.inc("d_s")
                sq_gate = (rdone, S.val("d_v"))
                dsq = S.val("d_s")

                def v_ln2(v, ttg=ttg, dsq=dsq):
                    v.wait_ge(S.hdl("d_s"), dsq)
                    v.reduce_sum(smallq[:, 0:1], q2y[:, 0:2048], AX.X)
                    v.drain()
                    v.reduce_sum(smallq[:, 1:2], r2[:, ttg, :], AX.X)
                    v.drain()
                    v.tensor_scalar_mul(smallq[:, 2:3], smallq[:, 1:2], 1.0 / DM)
                    v.drain()
                    v.tensor_mul(smallq[:, 3:4], smallq[:, 2:3], smallq[:, 2:3])
                    v.drain()
                    v.scalar_tensor_tensor(smallq[:, 4:5], smallq[:, 0:1],
                                           1.0 / DM, smallq[:, 3:4],
                                           ALU.mult, ALU.subtract)
                    v.drain()
                    v.tensor_scalar_add(smallq[:, 4:5], smallq[:, 4:5], EPS)
                    v.drain()
                    v.reciprocal(smallq[:, 5:6], smallq[:, 4:5])
                    v.drain().then_inc(*S.inc("d_v"))
                V(v_ln2)
                dln = S.val("d_v")
                inc_rt = S.inc("d_s")
                sc_ln2.append((ttg, sq_gate, inc_sq, dln, inc_rt))
                dls = S.val("d_s")

                def v_out(v, ttg=ttg, dls=dls, qo=S.val("q_out")):
                    v.wait_ge(S.hdl("d_s"), dls)
                    v.wait_ge(S.hdl("q_out"), qo)
                    v.tensor_scalar(outb[:], r2[:, ttg, :], smallq[:, 2:3],
                                    smallq[:, 5:6], ALU.subtract, ALU.mult)
                    v.drain().then_inc(*S.inc("d_v"))
                V(v_out)
                dvo = S.val("d_v")

                def y_out(y, ttg=ttg, dvo=dvo):
                    y.wait_ge(S.hdl("d_v"), dvo)
                    DMA(y, "q_out", out[ttg * 128:(ttg + 1) * 128, :], outb[:])
                SY(y_out)

            def s_ln2all(s, ops=list(sc_ln2), qq2_gate=qq2_gate):
                s.wait_ge(S.hdl("q_q2"), qq2_gate)
                for (ttg, sq_gate, inc_sq, dln, inc_rt) in ops:
                    s.wait_ge(S.hdl("d_v"), sq_gate[0])
                    s.wait_ge(S.hdl("d_v"), sq_gate[1])
                    s.activation(q2y[:, 0:2048], r2[:, ttg, :], AF.Square
                                 ).then_inc(*inc_sq)
                    s.wait_ge(S.hdl("d_v"), dln)
                    s.activation(smallq[:, 5:6], smallq[:, 5:6], AF.Sqrt
                                 ).then_inc(*inc_rt)
            SC(s_ln2all)

        for tt in range(4):
            emit_q2_tt(0, tt)
        emit_q2_flush(0)

        def cb0(dt_):
            emit_q2_tt(1, dt_)
            if dt_ == 3:
                emit_q2_flush(1)
        emit_mm2(0, cb0)
        emit_ln(0)
        emit_mm2(1)
        emit_ln(1)

    with nc.Block() as block:
        holder["b"] = block
        emit(stop_after)

        qend = [(n, S.val(n)) for n in
                ["q_out", "q_tap", "q_tt", "q_z", "q_nr", "q_sd", "q_q2", "q_gp"]
                if S.val(n) > 0]

        def g_end(g):
            for (n, v) in qend:
                g.wait_ge(S.hdl(n), v)
        G(g_end)

    return nc, taps


def make_inputs(core, ins):
    s, b = divmod(core, 4)
    u = np.asarray(ins["L0_attn_u"][s], np.float32)
    t0 = b * T
    ue = np.zeros((TE_, DM), np.float32)
    if b == 0:
        ue[PRE:] = u[t0:t0 + T]
    else:
        ue[:] = u[t0 - PRE:t0 + T]
    selv = np.zeros((1, 4), np.float32)
    selv[0, :b] = 1.0
    return {
        "uT": np.ascontiguousarray(ue.T),
        "u": ue,
        "W_in": np.asarray(ins["L0_attn_W_in"], np.float32),
        "conv_w": np.ascontiguousarray(
            np.asarray(ins["L0_attn_conv_w"], np.float32)
            .reshape(48, 128, 4).transpose(1, 0, 2)),
        "conv_b": np.ascontiguousarray(
            np.asarray(ins["L0_attn_conv_b"], np.float32)
            .reshape(48, 128).T),
        "dt_bias": np.asarray(ins["L0_attn_dt_bias"], np.float32).reshape(-1, 1),
        "A_log": np.asarray(ins["L0_attn_A_log"], np.float32).reshape(-1, 1),
        "Dv": np.asarray(ins["L0_attn_D"], np.float32).reshape(1, -1),
        "nw": np.asarray(ins["L0_attn_norm_weight"], np.float32).reshape(1, -1),
        "W_out": np.asarray(ins["L0_attn_W_out"], np.float32),
        "w1": np.ascontiguousarray(
            np.asarray(ins["L0_ffn_w1"], np.float32)
            [core * (HID // 8):(core + 1) * (HID // 8)]),
        "w2": np.ascontiguousarray(
            np.asarray(ins["L0_ffn_w2"], np.float32)
            [core * (DM // 8):(core + 1) * (DM // 8)]),
        "sel4": selv,
    }


def kernel(**inputs):
    nc, _ = build()
    in_maps = [make_inputs(c, inputs) for c in range(8)]
    res = run_bass_kernel_spmd(nc, in_maps, core_ids=list(range(8)))
    outp = np.zeros((2, 4096, DM), np.float32)
    for c in range(8):
        s, b = divmod(c, 4)
        outp[s, b * T:(b + 1) * T] = res.results[c]["out"]
    return outp
